# version 20
# baseline (speedup 1.0000x reference)
import contextlib

import numpy as np
import ml_dtypes

import concourse.bass as bass
import concourse.mybir as mybir
from concourse.bass_utils import run_bass_kernel_spmd

F32 = mybir.dt.float32
BF16 = mybir.dt.bfloat16
AF = mybir.ActivationFunctionType
ALU = mybir.AluOpType
BF = ml_dtypes.bfloat16

H = 32
KVH = 8
D = 128
CH = 128
TPC = 1024
NCH = TPC // CH
DM = 4096
NDC = DM // 128
NCORES = 8
EPS = 1e-5

ENGINES = ("tensor", "vector", "scalar", "gpsimd", "sync")
SEM_OWNER = {"pe": "tensor"}


class Emitter:

    def __init__(self, nc, sems):
        self.nc = nc
        self.sems = sems
        self.base = {k: 0 for k in sems}
        self.cnt = dict(self.base)
        self.cur = None
        self.eng = None
        self.lastwait = {}

    def start_pass(self, name, eng):
        self.cur = name
        self.eng = eng
        self.cnt = dict(self.base)
        self.lastwait = {}

    def commit(self):
        self.base = dict(self.cnt)

    def do(self, engine, fn, waits=(), inc=None, inc_n=1):
        if self.cur == engine:
            merged = {}
            for t in waits:
                if t is None:
                    continue
                for s, v in t.items():
                    if v > merged.get(s, 0):
                        merged[s] = v
            for s, v in sorted(merged.items()):
                if v > self.cnt[s]:
                    raise RuntimeError(
                        f"{engine} waits {s}>={v} > modeled {self.cnt[s]}")
                if SEM_OWNER.get(s) == engine:
                    continue
                if self.lastwait.get(s, -1) < v:
                    self.eng.wait_ge(self.sems[s], v)
                    self.lastwait[s] = v
            inst = fn(self.eng)
            if inc is not None:
                assert inst is not None
                inst.then_inc(self.sems[inc], inc_n)
        if inc is not None:
            self.cnt[inc] += inc_n
            return {inc: self.cnt[inc]}
        return {}

    def tok(self, sem):
        return {sem: self.cnt[sem]}


def merge_tok(*toks):
    m = {}
    for t in toks:
        if not t:
            continue
        for s, v in t.items():
            if v > m.get(s, 0):
                m[s] = v
    return m


class Bank:

    def __init__(self, ap):
        self.ap = ap
        self.writer = None
        self.readers = []

    def wr_waits(self):
        w = list(self.readers)
        if self.writer:
            w.append(self.writer)
        return w

    def set_writer(self, tok):
        self.writer = tok
        self.readers = []

    def rd_waits(self):
        return [self.writer] if self.writer else []

    def add_reader(self, tok):
        self.readers.append(tok)


class Builder:
    def __init__(self, taps=()):
        self.nc = bass.Bass("TRN2", target_bir_lowering=False)
        self.taps = set(taps)
        self.dT = {}
        self.es = contextlib.ExitStack()

    def din(self, name, shape, dt=F32):
        return self.nc.dram_tensor(name, list(shape), dt, kind="ExternalInput").ap()

    def dint(self, name, shape, dt=F32):
        return self.nc.dram_tensor(name, list(shape), dt)

    def tap(self, name, shape, dt=F32):
        if name in self.taps:
            self.dT[name] = self.nc.dram_tensor(
                "t_" + name, list(shape), dt, kind="ExternalOutput").ap()

    def sb(self, name, shape, dt, stack=None):
        return (stack or self.es).enter_context(
            self.nc.sbuf_tensor(name, list(shape), dt))

    def ps(self, name, stack=None):
        return Bank((stack or self.es).enter_context(
            self.nc.psum_tensor(name, [128, 512], F32)))

    def load_const(self, em, dst, src):
        return em.do("sync", lambda e: e.dma_start(out=dst, in_=src),
                     inc="l", inc_n=16)

    def run_block(self, body):
        with self.nc.Block() as blk:
            for name in ENGINES:
                def make(nm):
                    def f(eng):
                        self.em.start_pass(nm, eng)
                        body(self.em)
                    return f
                getattr(blk, name)(make(name))
        self.em.commit()

    def declare(self):
        nc = self.nc
        self.x_fm = self.din("x_fm", [4, 128, 8 * TPC])
        self.x_tm = self.din("x_tm", [TPC, DM])
        self.wq = self.din("wq", [H, 128, DM])
        self.wg = self.din("wg", [H, 128, DM])
        self.wk = self.din("wk", [2, 128, NDC * 512])
        self.wv = self.din("wv", [2, 128, NDC * 512])
        self.wo = self.din("wo", [8, 128, H * 512])
        self.wf = self.din("wf", [8, 128, NDC * 512])
        self.win = self.din("win", [128, NDC * 32])
        self.wa = self.din("wa", [128, NDC * 128])
        self.wb = self.din("wb", [8, 128, 512])
        self.fbias = self.din("fbias", [1, DM])
        self.dtb = self.din("dtb", [32, 1])
        self.alog = self.din("alog", [32, 1])
        self.gnw = self.din("gnw", [128, 1])
        self.lwc = self.din("lwc", [128, 1])
        self.cos_tm = self.din("cos_tm", [TPC, 128], BF16)
        self.sin_tm = self.din("sin_tm", [TPC, 128], BF16)
        self.cosT = self.din("cosT", [128, TPC], BF16)
        self.sinT = self.din("sinT", [128, TPC], BF16)
        self.identc = self.din("identc", [128, 128], BF16)
        self.identf = self.din("identf", [32, 32])
        self.negmaskc = self.din("negmaskc", [128, 128])
        self.hbc = self.din("hbc", [32, H * 128])
        self.ones32c = self.din("ones32c", [32, 128])
        self.onesrow = self.din("onesrow", [1, 512])
        self.onescol_bf = self.din("onescol_bf", [128, 1], BF16)
        self.epscol = self.din("epscol", [128, 1])
        self.onesrow_bf = self.din("onesrow_bf", [1, 128], BF16)
        self.mcol = self.din("mcol", [128, 4])
        self.invmcol = self.din("invmcol", [128, 4])
        self.out_t = nc.dram_tensor("out", [TPC, DM], F32, kind="ExternalOutput").ap()
        self.k_dram = self.dint("k_dram", [TPC, KVH * 128], BF16)
        self.v_dram = self.dint("v_dram", [TPC, KVH * 128], BF16)
        self.attn_dram = self.dint("attn_dram", [TPC, DM])
        self.x1_dram = self.dint("x1_dram", [TPC, DM])
        self.x1b_dram = self.dint("x1b_dram", [TPC, DM], BF16)
        self.ffn_dram = self.dint("ffn_dram", [TPC, DM])
        self.cc_in = self.dint("cc_in", [128, H * 129])
        self.cc_out = self.dint("cc_out", [4 * 128, H * 129])

        sb = self.sb
        self.xfm = sb("xfm", [128, NDC * TPC], BF16)
        self.states = sb("states", [128, H * 129], F32)
        self.s_gT = sb("s_gT", [32, TPC], F32)
        self.s_ngtm = sb("s_ngtm", [128, NCH * 32], F32)
        self.s_eglg = sb("s_eglg", [128, NCH * 32], F32)
        self.s_egl = sb("s_egl", [128, NCH * 32], F32)
        self.s_dttm = sb("s_dttm", [128, NCH * 32], F32)
        self.s_negm = sb("s_negm", [128, 128], F32)
        self.s_ident = sb("s_ident", [128, 128], BF16)
        self.s_gnw = sb("s_gnw", [128, 1], F32)
        self._s_eps2 = sb("s_eps2", [128, 1], F32)
        self.s_mP = sb("s_mP", [128, 4], F32)
        self.s_imP = sb("s_imP", [128, 4], F32)
        self.s_lwB = sb("s_lwB", [128, 1], F32)

        semnames = ["w", "l", "s", "s2", "pe", "dve", "act", "pool", "cc"]
        self.sems = {n: self.es.enter_context(self.nc.semaphore("sm_" + n))
                     for n in semnames}
        self.em = Emitter(self.nc, self.sems)

        self.tap("dtT", [32, TPC])
        self.tap("zb", [32, TPC])
        self.tap("sp1", [32, TPC])
        self.tap("sp2", [32, TPC])
        self.tap("GT", [32, TPC])
        self.tap("payload", [128, H * 129])
        self.tap("sin", [128, H * 129])
        self.tap("kd", [TPC, KVH * 128], BF16)
        self.tap("vd", [TPC, KVH * 128], BF16)
        self.tap("yall", [128, H * TPC], BF16)
        self.tap("yallC", [128, H * TPC], BF16)
        self.tap("gate0", [128, TPC], BF16)
        self.tap("dcy0", [128, NCH * 128])
        self.tap("sinB", [128, H * 129])
        self.tap("st0", [128, NCH * 128])
        self.tap("qfm0", [128, TPC], BF16)
        self.tap("mtb2", [128, 128], BF16)
        self.tap("y2b0", [128, 512], BF16)
        self.tap("rsbf0", [1, 512], BF16)
        self.tap("rbbf0", [128, 512], BF16)
        self.tap("egr0", [128, NCH * 128], BF16)
        self.tap("ffn", [TPC, DM])
        self.tap("xfmE", [128, NDC * TPC], BF16)
        self.tap("attn", [TPC, DM])
        self.tap("x1", [TPC, DM])
        self.tap("gtm", [128, NCH * 32])
        self.tap("eglg", [128, NCH * 32])
        self.tap("egl", [128, NCH * 32])

    def phase0(self):
        b = self

        def body(em):
            for dst, src in [(b.xfm, None)]:
                pass
            b.xw = []
            for i in range(4):
                b.xw.append(em.do("gpsimd", lambda e, i=i: e.dma_start(
                    out=b.xfm[:, i * 8 * TPC:(i + 1) * 8 * TPC],
                    in_=b.x_fm[i]),
                    inc="w", inc_n=16))
            for dst, src in [(b.s_negm[:, :], b.negmaskc), (b.s_ident[:, :], b.identc),
                             (b.s_gnw[:, :], b.gnw), (b._s_eps2[:, :], b.epscol),
                             (b.s_lwB[:, :], b.lwc)]:
                self.load_const(em, dst, src)
            b.pinit = em.do("gpsimd", lambda e: e.memset(b.states[:, :], 0.0), inc="pool")
            b.constok = em.tok("l")
        self.run_block(body)

    def phaseA(self, level=4):
        b = self
        st = contextlib.ExitStack()
        sb = lambda n, shp, dt: self.sb(n, shp, dt, st)
        s_identf = sb("s_identf", [32, 32], F32)
        s_costm = sb("s_costm", [128, NCH, 128], BF16)
        s_sintm = sb("s_sintm", [128, NCH, 128], BF16)
        s_dtb = sb("s_dtb", [32, 1], F32)
        s_alog = sb("s_alog", [32, 1], F32)
        s_negA = sb("s_negA", [32, 1], F32)
        s_dtT = sb("s_dtT", [32, TPC], F32)
        s_z32 = sb("s_z32", [32, 128], F32)
        s_gtm = sb("s_gtm", [128, NCH * 32], F32)
        s_gacc = sb("s_gacc", [128, 32], F32)
        s_o32 = sb("s_o32", [32, 128], F32)
        winb = sb("winb", [128, NDC * 32], BF16)
        wkb_ = sb("wkb", [128, 2, 8 * 512], BF16)
        wkb = lambda i: wkb_[:, i]
        kcb_ = sb("kcb", [128, 4, KVH * 128], BF16)
        kcb = lambda i: kcb_[:, i]
        vcb_ = sb("vcb", [128, 4, KVH * 128], BF16)
        vcb = lambda i: vcb_[:, i]
        xcb_ = sb("xcb", [128, 2, 128], BF16)
        xcb = lambda i: xcb_[:, i]
        kdb_ = sb("kdb", [128, 2, 128], BF16)
        kdb = lambda i: kdb_[:, i]
        rotb = sb("rotb", [128, 512], F32)
        ktmp = sb("ktmp", [128, 512], F32)
        spT_ = sb("spT", [32, 3, 512], F32)
        spT = lambda i: spT_[:, i]
        diag = sb("diag", [32, 32], F32)
        glmg = sb("glmg", [128, 32], F32)
        p_pr = [self.ps(f"p_pr{i}", st) for i in range(4)]
        p_cs = [self.ps(f"p_cs{i}", st) for i in range(2)]
        p_sm = [self.ps(f"p_sm{i}", st) for i in range(2)]

        def body(em):
            cl0 = b.constok
            for dst, src in [(s_identf[:, :], b.identf), (s_dtb[:, :], b.dtb),
                             (s_alog[:, :], b.alog), (b.s_mP[:, :], b.mcol),
                             (b.s_imP[:, :], b.invmcol), (s_o32[:, :], b.ones32c)]:
                self.load_const(em, dst, src)
            self.load_const(em, s_costm[:, :, :],
                            b.cos_tm.rearrange("(c p) e -> p c e", p=128))
            self.load_const(em, s_sintm[:, :, :],
                            b.sin_tm.rearrange("(c p) e -> p c e", p=128))
            zinit = em.do("gpsimd", lambda e: e.memset(s_z32[:, :], 0.0), inc="pool")
            gainit = em.do("gpsimd", lambda e: e.memset(s_gacc[:, :], 0.0), inc="pool")
            cl = em.tok("l")
            xw = merge_tok(*b.xw)
            xws = b.xw

            wt = em.do("gpsimd", lambda e: e.dma_start(out=winb[:, :], in_=b.win),
                       inc="w", inc_n=16)
            dt_done = []
            prev_t1 = None
            prev_t5 = None
            prev_zbtap = None
            for ts in range(2):
                bank = p_cs[ts]
                mm = None
                for d in range(NDC):
                    mm = em.do("tensor", lambda e, d=d, ts=ts, bank=bank: e.matmul(
                        bank.ap[0:32, :],
                        winb[:, d * 32:(d + 1) * 32],
                        b.xfm[:, d * TPC + ts * 512: d * TPC + (ts + 1) * 512],
                        start=(d == 0), stop=(d == NDC - 1)),
                        waits=[xws[d // 8], wt] + (bank.wr_waits() if d == 0 else []), inc="pe")
                bank.set_writer(mm)
                t0 = em.do("vector", lambda e, bank=bank: e.tensor_scalar(
                    spT(0)[:, :], bank.ap[0:32, :], s_dtb[:, :], None, ALU.add),
                    waits=[cl, prev_t1, prev_zbtap] + bank.rd_waits(), inc="dve")
                bank.add_reader(t0)
                zbtap = None
                if "zb" in b.taps:
                    zbtap = em.do("sync", lambda e, ts=ts: e.dma_start(
                        out=b.dT["zb"][:, ts * 512:(ts + 1) * 512], in_=spT(0)[:, :]),
                        waits=[t0], inc="s", inc_n=16)
                t1 = em.do("scalar", lambda e: e.activation(
                    spT(1)[:, :], spT(0)[:, :], AF.Abs),
                    waits=[t0, prev_t5], inc="act")
                prev_t1 = t1
                t2 = em.do("scalar", lambda e: e.activation(
                    spT(1)[:, :], spT(1)[:, :], AF.Exp, scale=-1.0),
                    waits=[t1], inc="act")
                t3 = em.do("scalar", lambda e: e.activation(
                    spT(1)[:, :], spT(1)[:, :], AF.Ln, bias=1.0),
                    waits=[t2], inc="act")
                t4 = em.do("vector", lambda e: e.tensor_relu(
                    spT(2)[:, :], spT(0)[:, :]),
                    waits=[t0, t2], inc="dve")
                w_extra = []
                if "sp1" in b.taps:
                    w_extra.append(em.do("sync", lambda e, ts=ts: e.dma_start(
                        out=b.dT["sp1"][:, ts * 512:(ts + 1) * 512], in_=spT(1)[:, :]),
                        waits=[t3], inc="s", inc_n=16))
                if "sp2" in b.taps:
                    w_extra.append(em.do("sync", lambda e, ts=ts: e.dma_start(
                        out=b.dT["sp2"][:, ts * 512:(ts + 1) * 512], in_=spT(2)[:, :]),
                        waits=[t4], inc="s", inc_n=16))
                t5 = em.do("vector", lambda e, ts=ts: e.tensor_add(
                    s_dtT[:, ts * 512:(ts + 1) * 512], spT(2)[:, :], spT(1)[:, :]),
                    waits=[t3, t4] + w_extra, inc="dve")
                prev_t5 = t5
                prev_zbtap = zbtap
                dt_done.append(t5)
            a1 = em.do("scalar", lambda e: e.activation(
                s_negA[:, :], s_alog[:, :], AF.Exp), waits=[cl], inc="act")
            a2 = em.do("vector", lambda e: e.tensor_scalar(
                s_negA[:, :], s_negA[:, :], -1.0, None, ALU.mult),
                waits=[a1], inc="dve")
            gT = em.do("vector", lambda e: e.tensor_scalar(
                b.s_gT[:, :], s_dtT[:, :], s_negA[:, :], None, ALU.mult),
                waits=dt_done + [a2], inc="dve")
            if "dtT" in b.taps:
                em.do("sync", lambda e: e.dma_start(out=b.dT["dtT"], in_=s_dtT[:, :]),
                      waits=[gT], inc="s", inc_n=16)

            gm = []
            self._gmst = dict(last_sc=gT, glmm_prev=None, e3_prev=None)

            def emit_gmach(c):
                last_sc = self._gmst["last_sc"]
                glmm_prev = self._gmst["glmm_prev"]
                e3_prev = self._gmst["e3_prev"]
                sc = em.do("vector", lambda e, c=c: e.tensor_tensor_scan(
                    b.s_gT[:, c * 128:(c + 1) * 128],
                    b.s_gT[:, c * 128:(c + 1) * 128], s_z32[:, :],
                    0.0, ALU.add, ALU.add), waits=[last_sc, zinit], inc="dve")
                bank = p_sm[c % 2]
                tp1 = em.do("tensor", lambda e, c=c, bank=bank: e.transpose(
                    bank.ap[:, 0:32], b.s_gT[:, c * 128:(c + 1) * 128],
                    s_identf[:, :]), waits=[sc, cl] + bank.wr_waits(), inc="pe")
                bank.set_writer(tp1)
                cp1 = em.do("scalar", lambda e, c=c, bank=bank: e.copy(
                    s_gtm[:, c * 32:(c + 1) * 32], bank.ap[:, 0:32]),
                    waits=bank.rd_waits(), inc="act")
                bank.add_reader(cp1)
                tp2 = em.do("tensor", lambda e, c=c, bank=bank: e.transpose(
                    bank.ap[:, 64:96], s_dtT[:, c * 128:(c + 1) * 128],
                    s_identf[:, :]),
                    waits=[dt_done[c // 4], cp1] + bank.wr_waits(), inc="pe")
                bank.set_writer(tp2)
                cp2 = em.do("scalar", lambda e, c=c, bank=bank: e.copy(
                    b.s_dttm[:, c * 32:(c + 1) * 32], bank.ap[:, 64:96]),
                    waits=bank.rd_waits(), inc="act")
                bank.add_reader(cp2)
                ng = em.do("scalar", lambda e, c=c: e.activation(
                    b.s_ngtm[:, c * 32:(c + 1) * 32],
                    s_gtm[:, c * 32:(c + 1) * 32], AF.Copy, scale=-1.0),
                    waits=[cp1], inc="act")
                dg = em.do("vector", lambda e, c=c: e.tensor_scalar(
                    diag[:, :], s_identf[:, :],
                    b.s_gT[:, c * 128 + 127: c * 128 + 128], None, ALU.mult),
                    waits=[sc, cl, glmm_prev], inc="dve")
                bank2 = p_sm[(c + 1) % 2]
                glmm = em.do("tensor", lambda e, bank2=bank2: e.matmul(
                    bank2.ap[:, 96:128], s_o32[:, :], diag[:, :],
                    start=True, stop=True),
                    waits=[dg, cl] + bank2.wr_waits(), inc="pe")
                bank2.set_writer(glmm)
                e1 = em.do("scalar", lambda e, c=c, bank2=bank2: e.activation(
                    b.s_egl[:, c * 32:(c + 1) * 32], bank2.ap[:, 96:128], AF.Exp),
                    waits=bank2.rd_waits(), inc="act")
                bank2.add_reader(e1)
                e2 = em.do("vector", lambda e, c=c, bank2=bank2: e.tensor_sub(
                    glmg[:, :], bank2.ap[:, 96:128],
                    s_gtm[:, c * 32:(c + 1) * 32]),
                    waits=[cp1, e3_prev] + bank2.rd_waits(), inc="dve")
                bank2.add_reader(e2)
                e3 = em.do("scalar", lambda e, c=c: e.activation(
                    b.s_eglg[:, c * 32:(c + 1) * 32], glmg[:, :], AF.Exp),
                    waits=[e2], inc="act")
                e4 = em.do("vector", lambda e, c=c, bank2=bank2: e.tensor_add(
                    s_gacc[:, :], s_gacc[:, :], bank2.ap[:, 96:128]),
                    waits=[gainit] + bank2.rd_waits(), inc="dve")
                bank2.add_reader(e4)
                gm.append(dict(dttm=cp2, ng=ng, egl=e1, eglg=e3, gacc=e4, gtm=cp1))
                self._gmst.update(last_sc=sc, glmm_prev=glmm, e3_prev=e3)

            NBLK = 32

            def wsrc(bi):
                return (bi // 16, (bi // 8) % 2, (bi // 4) % 2, bi % 4)

            def wload(bi, deps):
                half, which, es_, dq = wsrc(bi)
                src = b.wk if which == 0 else b.wv
                return em.do("gpsimd", lambda e, bi=bi, es_=es_, dq=dq, src=src: e.dma_start(
                    out=wkb(bi % 2)[:, :],
                    in_=src[es_, :, dq * 8 * 512:(dq + 1) * 8 * 512]),
                    waits=deps, inc="w", inc_n=16)

            for c_ in range(NCH):
                emit_gmach(c_)
            wtok = {0: wload(0, []), 1: wload(1, [])}
            wlast = {0: None, 1: None}
            spills = []
            kvbuf_free = [None] * 4
            prev_rope = None
            for half in range(2):
                kv_done = [[] for _ in range(4)]
                for blk in range(16):
                    bi = half * 16 + blk
                    _, which, es_, dq = wsrc(bi)
                    slot = bi % 2
                    mm_last = None
                    for d8 in range(8):
                        d = dq * 8 + d8
                        for ci in range(4):
                            c = half * 4 + ci
                            bank = p_pr[ci]
                            w0 = [xws[d // 8], wtok[bi]]
                            if d == 0:
                                w0 += bank.wr_waits()
                            mm_last = em.do("tensor", lambda e, d=d, d8=d8, c=c, slot=slot, bank=bank: e.matmul(
                                bank.ap[:, :],
                                b.xfm[:, d * TPC + c * 128: d * TPC + (c + 1) * 128],
                                wkb(slot)[:, d8 * 512:(d8 + 1) * 512],
                                start=(d == 0), stop=(d == NDC - 1)),
                                waits=w0, inc="pe")
                    wlast[slot] = mm_last
                    if bi + 2 < NBLK:
                        wtok[bi + 2] = wload(bi + 2, [wlast[slot]])
                    if dq == 3:
                        import os as _os
                        krope = _os.environ.get("KROPE", "1") == "1"
                        for ci in range(4):
                            c = half * 4 + ci
                            bank = p_pr[ci]
                            bank.set_writer(em.tok("pe"))
                            if which == 1 or not krope:
                                dstb = vcb if which == 1 else kcb
                                tv = em.do("scalar", lambda e, ci=ci, es_=es_, bank=bank, dstb=dstb: e.copy(
                                    dstb(ci)[:, es_ * 512:(es_ + 1) * 512], bank.ap[:, :]),
                                    waits=bank.rd_waits() + [kvbuf_free[ci]], inc="act")
                                bank.add_reader(tv)
                                kv_done[ci].append(tv)
                            else:
                                ktv = ktmp[:, :].rearrange("p (bb e) -> p bb e", bb=4)
                                rv = rotb[:, :].rearrange("p (bb e) -> p bb e", bb=4)
                                e0 = em.do("scalar", lambda e, bank=bank: e.copy(
                                    ktmp[:, :], bank.ap[:, :]),
                                    waits=bank.rd_waits() + [prev_rope], inc="act")
                                bank.add_reader(e0)
                                r1 = em.do("scalar", lambda e, ktv=ktv, rv=rv: e.activation(
                                    rv[:, :, 0:64], ktv[:, :, 64:128], AF.Copy, scale=-1.0),
                                    waits=[e0, prev_rope], inc="act")
                                r2 = em.do("scalar", lambda e, ktv=ktv, rv=rv: e.copy(
                                    rv[:, :, 64:128], ktv[:, :, 0:64]),
                                    waits=[e0, r1], inc="act")
                                cosr = s_costm[:, c].unsqueeze(1).broadcast_to([128, 4, 128])
                                sinr = s_sintm[:, c].unsqueeze(1).broadcast_to([128, 4, 128])
                                t1 = em.do("vector", lambda e, ktv=ktv, cosr=cosr: e.tensor_tensor(
                                    ktv, ktv, cosr, ALU.mult),
                                    waits=[e0, r1, r2, cl], inc="dve")
                                t2 = em.do("vector", lambda e, rv=rv, sinr=sinr: e.tensor_tensor(
                                    rv, rv, sinr, ALU.mult),
                                    waits=[r1, r2, cl], inc="dve")
                                t3 = em.do("vector", lambda e, ci=ci, es_=es_: e.tensor_add(
                                    kcb(ci)[:, es_ * 512:(es_ + 1) * 512],
                                    ktmp[:, :], rotb[:, :]),
                                    waits=[t1, t2, kvbuf_free[ci]], inc="dve")
                                prev_rope = t3
                                kv_done[ci].append(t3)
                for ci in range(4):
                    c = half * 4 + ci
                    kvd = kv_done[ci]
                    s1 = em.do("sync", lambda e, ci=ci, c=c: e.dma_start(
                        out=b.k_dram[c * 128:(c + 1) * 128, :], in_=kcb(ci)[:, :]),
                        waits=kvd, inc="s", inc_n=16)
                    s2 = em.do("sync", lambda e, ci=ci, c=c: e.dma_start(
                        out=b.v_dram[c * 128:(c + 1) * 128, :], in_=vcb(ci)[:, :]),
                        waits=kvd, inc="s", inc_n=16)
                    spills += [s1, s2]
                    for h in (range(H) if level >= 3 else []):
                        kv = h // 4
                        par = h % 2
                        bank = p_cs[par]
                        xc = em.do("vector", lambda e, ci=ci, kv=kv, c=c, h=h, par=par: e.tensor_scalar(
                            xcb(par)[:, :], vcb(ci)[:, kv * 128:(kv + 1) * 128],
                            b.s_dttm[:, c * 32 + h: c * 32 + h + 1], None, ALU.mult),
                            waits=kvd + [gm[c]["dttm"]] + ([bank.writer] if bank.writer else []),
                            inc="dve")
                        kd = em.do("vector", lambda e, ci=ci, kv=kv, c=c, h=h, par=par: e.tensor_scalar(
                            kdb(par)[:, :], kcb(ci)[:, kv * 128:(kv + 1) * 128],
                            b.s_eglg[:, c * 32 + h: c * 32 + h + 1], None, ALU.mult),
                            waits=kvd + [gm[c]["eglg"]], inc="dve")
                        cs = em.do("tensor", lambda e, par=par, bank=bank: e.matmul(
                            bank.ap[:, 0:128], kdb(par)[:, :], xcb(par)[:, :],
                            start=True, stop=True),
                            waits=[xc, kd] + bank.wr_waits(), inc="pe")
                        bank.set_writer(cs)
                        stt = em.do("vector", lambda e, c=c, h=h, bank=bank: e.scalar_tensor_tensor(
                            b.states[:, h * 129: h * 129 + 128],
                            b.states[:, h * 129: h * 129 + 128],
                            b.s_egl[:, c * 32 + h: c * 32 + h + 1],
                            bank.ap[:, 0:128], ALU.mult, ALU.add),
                            waits=[gm[c]["egl"], b.pinit] + bank.rd_waits(), inc="dve")
                        bank.add_reader(stt)
                    kvbuf_free[ci] = merge_tok(s1, s2, em.tok("dve"))

            for tn, key in [("GT", None), ("gtm", "gtm"), ("eglg", "eglg"), ("egl", "egl")]:
                if tn in b.taps:
                    srcs = {"GT": b.s_gT, "gtm": s_gtm, "eglg": b.s_eglg, "egl": b.s_egl}
                    w = [self._gmst["last_sc"]] if key is None else [t[key] for t in gm]
                    em.do("sync", lambda e, tn=tn, srcs=srcs: e.dma_start(
                        out=b.dT[tn], in_=srcs[tn][:, :]), waits=w, inc="s", inc_n=16)

            if level < 2:
                em.do("sync", lambda e: e.nop(),
                      waits=[em.tok("s"), em.tok("l"), em.tok("w")])
                return
            eg = em.do("scalar", lambda e: e.activation(
                s_gacc[:, :], s_gacc[:, :], AF.Exp),
                waits=[gm[NCH - 1]["gacc"]], inc="act")
            dcw = em.do("vector", lambda e: e.tensor_copy(
                b.states[:, :].rearrange("p (h q) -> p h q", h=H)[:, :, 128:129],
                s_gacc[:, :].unsqueeze(2)),
                waits=[eg, em.tok("dve")], inc="dve")
            if "payload" in b.taps:
                em.do("sync", lambda e: e.dma_start(out=b.dT["payload"], in_=b.states[:, :]),
                      waits=[dcw], inc="s", inc_n=16)
            if level < 4:
                for tn in ("kd", "vd"):
                    if tn in b.taps:
                        em.do("sync", lambda e, tn=tn: e.dma_start(
                            out=b.dT[tn], in_=(b.k_dram if tn == "kd" else b.v_dram)[:, :]),
                            waits=spills, inc="s", inc_n=16)
                em.do("sync", lambda e: e.nop(),
                      waits=[em.tok("s"), em.tok("l"), em.tok("w")])
                return
            pw = em.do("sync", lambda e: e.dma_start(out=b.cc_in[:, :], in_=b.states[:, :]),
                       waits=[dcw], inc="s", inc_n=16)
            z2 = em.do("gpsimd", lambda e: e.memset(b.states[:, :], 0.0),
                       waits=[pw], inc="pool")
            cct = em.do("gpsimd", lambda e: e.collective_compute(
                "AllGather", ALU.bypass,
                replica_groups=[[0, 1, 2, 3], [4, 5, 6, 7]],
                ins=[b.cc_in.ap().opt()], outs=[b.cc_out.ap().opt()]),
                waits=[pw], inc="cc")
            b.cct_tok = cct
            b.z2_tok = z2
            b.mask_cl = em.tok("l")
            if "kd" in b.taps:
                em.do("sync", lambda e: e.dma_start(out=b.dT["kd"], in_=b.k_dram[:, :]),
                      waits=spills, inc="s", inc_n=16)
            if "vd" in b.taps:
                em.do("sync", lambda e: e.dma_start(out=b.dT["vd"], in_=b.v_dram[:, :]),
                      waits=spills, inc="s", inc_n=16)
            em.do("sync", lambda e: e.nop(),
                  waits=[em.tok("s"), em.tok("l"), em.tok("w")])
        self.run_block(body)
        st.close()

    def phaseB(self, yall):
        b = self
        st = contextlib.ExitStack()
        sb = lambda n, shp, dt: self.sb(n, shp, dt, st)
        s_hb_ = sb("s_hb", [32, 2, 128], F32)
        s_hb = lambda i: s_hb_[:, i]
        s_cosT = sb("s_cosT", [128, TPC], BF16)
        s_sinT = sb("s_sinT", [128, TPC], BF16)
        s_oc = sb("s_oc", [128, 1], BF16)
        s_orb = sb("s_orb", [1, 128], BF16)
        k_kv_ = sb("k_kv", [128, 2, NCH, 128], BF16)
        k_kv = lambda i: k_kv_[:, i]
        v_kv_ = sb("v_kv", [128, 1, NCH, 128], BF16)
        v_kv = lambda i: v_kv_[:, 0]
        kfm_ = sb("kfm", [128, 2, TPC], BF16)
        kfm = lambda i: kfm_[:, i]
        wqb_ = sb("wqb", [128, 2, 2048], BF16)
        wqb = lambda i: wqb_[:, i]
        qfm_ = sb("qfm", [128, 2, TPC], BF16)
        qfm = lambda i: qfm_[:, i]
        qtmp = sb("qtmp", [128, 512], F32)
        qrot = sb("qrot", [128, 512], F32)
        dcy_ = sb("dcy", [128, 2, NCH * 128], F32)
        dcy = lambda i: dcy_[:, i]
        egr_ = sb("egr", [128, 2, NCH * 128], BF16)
        egr = lambda i: egr_[:, i]
        mtb_ = sb("mtb", [128, 2, 128], BF16)
        mtb = lambda i: mtb_[:, i]
        qgb_ = sb("qgb", [128, 2, 128], BF16)
        qgb = lambda i: qgb_[:, i]
        xcb_ = sb("xcbB", [128, 2, 128], BF16)
        xcb = lambda i: xcb_[:, i]
        kdb_ = sb("kdbB", [128, 2, 128], BF16)
        kdb = lambda i: kdb_[:, i]
        stb_ = sb("stb", [128, 2, 128], BF16)
        stb = lambda i: stb_[:, i]
        gbuf = sb("gbuf", [128, (H // 8) * 129], F32)
        dcolb = sb("dcolb", [128, 1], F32)
        y2b = sb("y2b", [128, 512], BF16)
        rsrow = sb("rsrow", [1, 512], F32)
        rsbf = sb("rsbf", [1, 512], BF16)
        rbbf = sb("rbbf", [128, 512], BF16)
        p_q = [self.ps(f"pB_q{i}", st) for i in range(2)]
        p_y = [self.ps(f"pB_y{i}", st) for i in range(2)]
        p_g_ = self.ps("pB_g", st)
        p_g = [p_g_, p_g_]
        p_gn = self.ps("pB_gn", st)
        p_st = self.ps("pB_st", st)
        p_cs = self.ps("pB_cs", st)

        def body(em):
            for dst, srcd in [(s_cosT[:, :], b.cosT),
                              (s_sinT[:, :], b.sinT), (s_oc[:, :], b.onescol_bf),
                              (s_orb[:, :], b.onesrow_bf)]:
                self.load_const(em, dst, srcd)
            cl = em.tok("l")
            prev_dve = None
            HH = H // 8
            for r in range(4):
                for sub in range(8):
                    gld = em.do("sync", lambda e, r=r, sub=sub: e.dma_start(
                        out=gbuf[:, :],
                        in_=b.cc_out[r * 128:(r + 1) * 128,
                                     sub * HH * 129:(sub + 1) * HH * 129]),
                        waits=[b.cct_tok, prev_dve], inc="l", inc_n=16)
                    mk = em.do("vector", lambda e, r=r: e.tensor_scalar(
                        gbuf[:, :], gbuf[:, :],
                        b.s_mP[:, r:r + 1], None, ALU.mult),
                        waits=[gld, b.mask_cl], inc="dve")
                    for hh in range(HH):
                        h = sub * HH + hh
                        dc = em.do("vector", lambda e, hh=hh, r=r: e.tensor_scalar(
                            dcolb[:, :], gbuf[:, hh * 129 + 128: hh * 129 + 129],
                            b.s_imP[:, r:r + 1], None, ALU.add), waits=[mk], inc="dve")
                        em.do("vector", lambda e, hh=hh, h=h: e.scalar_tensor_tensor(
                            b.states[:, h * 129: h * 129 + 128],
                            b.states[:, h * 129: h * 129 + 128],
                            dcolb[:, :],
                            gbuf[:, hh * 129: hh * 129 + 128],
                            ALU.mult, ALU.add), waits=[dc, b.z2_tok], inc="dve")
                    prev_dve = em.tok("dve")
            b.sin_done = em.tok("dve")
            sin_tok = b.sin_done
            if "sinB" in b.taps:
                em.do("sync", lambda e: e.dma_start(out=b.dT["sinB"], in_=b.states[:, :]),
                      waits=[sin_tok], inc="s", inc_n=16)

            qctr = [0]
            wqtok = {}
            wqcons = {0: None, 1: None}

            def wq_load():
                i = qctr[0]
                qctr[0] += 1
                h, hf = divmod(i, 2)
                slot = i % 2
                wqtok[i] = em.do("gpsimd", lambda e, h=h, hf=hf, slot=slot: e.dma_start(
                    out=wqb(slot)[:, :], in_=b.wq[h, :, hf * 2048:(hf + 1) * 2048]),
                    waits=[wqcons[slot]], inc="w", inc_n=16)
            wq_load()
            wq_load()

            yg_toks = []

            def emit_qslice(h, sl8):
                for d in range(sl8 * 4, sl8 * 4 + 4):
                    i = h * 2 + d // 16
                    slot = i % 2
                    for ts in range(2):
                        bank = p_q[ts]
                        w0 = [wqtok[i]]
                        if d == 0:
                            w0 += bank.wr_waits()
                        self._qmm[ts] = em.do("tensor", lambda e, d=d, ts=ts, slot=slot, bank=bank: e.matmul(
                            bank.ap[:, :],
                            wqb(slot)[:, (d % 16) * 128:(d % 16 + 1) * 128],
                            b.xfm[:, d * TPC + ts * 512: d * TPC + (ts + 1) * 512],
                            start=(d == 0), stop=(d == NDC - 1)),
                            waits=w0, inc="pe")
                if sl8 in (3, 7):
                    i = h * 2 + sl8 // 4
                    wqcons[i % 2] = self._qmm[1]
                    if qctr[0] < H * 2:
                        wq_load()

            def emit_rope(h):
                hp = h % 2
                rope_done = []
                for ts in range(2):
                    bank = p_q[ts]
                    bank.set_writer(self._qmm[ts])
                    e0 = em.do("scalar", lambda e, bank=bank: e.copy(
                        qtmp[:, :], bank.ap[:, :]),
                        waits=bank.rd_waits() + [self._qrope], inc="act")
                    bank.add_reader(e0)
                    r1 = em.do("scalar", lambda e: e.activation(
                        qrot[0:64, :], qtmp[64:128, :], AF.Copy, scale=-1.0),
                        waits=[e0, self._qrope], inc="act")
                    r2 = em.do("scalar", lambda e: e.copy(
                        qrot[64:128, :], qtmp[0:64, :]), waits=[e0, r1], inc="act")
                    t1 = em.do("vector", lambda e, ts=ts: e.tensor_tensor(
                        qtmp[:, :], qtmp[:, :],
                        s_cosT[:, ts * 512:(ts + 1) * 512], ALU.mult),
                        waits=[e0, r1, r2, cl], inc="dve")
                    t2 = em.do("vector", lambda e, ts=ts: e.tensor_tensor(
                        qrot[:, :], qrot[:, :],
                        s_sinT[:, ts * 512:(ts + 1) * 512], ALU.mult),
                        waits=[r1, r2, cl], inc="dve")
                    t3 = em.do("vector", lambda e, ts=ts, hp=hp: e.tensor_add(
                        qfm(hp)[:, ts * 512:(ts + 1) * 512], qtmp[:, :], qrot[:, :]),
                        waits=[t1, t2, self._qfree[hp]], inc="dve")
                    self._qrope = t3
                    rope_done.append(t3)
                return rope_done

            def emit_decay(h, c):
                hp = h % 2
                gb = p_g[0]
                hbs = s_hb(hp)
                gw = em.do("tensor", lambda e, c=c, h=h, hbs=hbs, gb=gb: e.matmul(
                    gb.ap[:, 0:128], hbs[:, :],
                    b.s_gT[:, c * 128:(c + 1) * 128],
                    start=True, stop=True),
                    waits=[cl, self._hbt[hp]] + gb.wr_waits(), inc="pe")
                gb.set_writer(gw)
                dm = em.do("vector", lambda e, c=c, hp=hp, gb=gb: e.tensor_add(
                    dcy(hp)[:, c * 128:(c + 1) * 128], gb.ap[:, 0:128], b.s_negm[:, :]),
                    waits=gb.rd_waits() + [self._dcyfree[hp]], inc="dve")
                gb.add_reader(dm)
                dk = em.do("scalar", lambda e, c=c, h=h, hp=hp: e.activation(
                    dcy(hp)[:, c * 128:(c + 1) * 128],
                    dcy(hp)[:, c * 128:(c + 1) * 128], AF.Exp,
                    bias=b.s_ngtm[:, c * 32 + h: c * 32 + h + 1]),
                    waits=[dm], inc="act")
                eg2 = em.do("scalar", lambda e, c=c, hp=hp, gb=gb: e.activation(
                    egr(hp)[:, c * 128:(c + 1) * 128], gb.ap[:, 0:128], AF.Exp),
                    waits=gb.rd_waits() + [self._egrfree[hp]], inc="act")
                gb.add_reader(eg2)
                self._dk[(h, c)] = dk
                self._eg2[(h, c)] = eg2

            def load_hb(h):
                hp = h % 2
                self._hbt[hp] = em.do("sync", lambda e, h=h, hp=hp: e.dma_start(
                    out=s_hb(hp)[:, :], in_=b.hbc[:, h * 128:(h + 1) * 128]),
                    waits=[self._hbfree[hp]], inc="l", inc_n=16)

            def gn_pump():
                if not self._gnq:
                    return
                j = self._gnq[0]
                s = j["st"]
                j["st"] += 1
                yb = j["yb"]
                if s == 0:
                    j["sq"] = em.do("scalar", lambda e, yb=yb: e.activation(
                        y2b[:, :], yb.ap[:, :], AF.Square),
                        waits=yb.rd_waits() + [self._y2free], inc="act")
                    yb.add_reader(j["sq"])
                elif s == 1:
                    j["smq"] = em.do("tensor", lambda e: e.matmul(
                        p_gn.ap[0:1, :], s_oc[:, :], y2b[:, :],
                        start=True, stop=True),
                        waits=[j["sq"], cl] + p_gn.wr_waits(), inc="pe")
                    p_gn.set_writer(j["smq"])
                    self._y2free = j["smq"]
                    rs1 = em.do("scalar", lambda e: e.activation(
                        rsrow[:, :], p_gn.ap[0:1, :], AF.Ln,
                        scale=1.0 / 128.0, bias=b._s_eps2[0:1, :]),
                        waits=p_gn.rd_waits() + [self._rsfree], inc="act")
                    p_gn.add_reader(rs1)
                    j["rs2"] = em.do("scalar", lambda e: e.activation(
                        rsbf[:, :], rsrow[:, :], AF.Exp, scale=-0.5),
                        waits=[rs1, self._rsbffree], inc="act")
                    self._rsfree = j["rs2"]
                else:
                    bcm = em.do("tensor", lambda e: e.matmul(
                        p_gn.ap[:, :], s_orb[:, :], rsbf[:, :],
                        start=True, stop=True),
                        waits=[j["rs2"], cl] + p_gn.wr_waits(), inc="pe")
                    p_gn.set_writer(bcm)
                    self._rsbffree = bcm
                    rb = em.do("scalar", lambda e: e.activation(
                        rbbf[:, :], p_gn.ap[:, :], AF.Copy),
                        waits=p_gn.rd_waits() + [self._rbfree], inc="act")
                    p_gn.add_reader(rb)
                    yfin = em.do("vector", lambda e, h=j["h"], cg=j["cg"], yb=yb: e.scalar_tensor_tensor(
                        yall[:, h * TPC + cg * 512: h * TPC + (cg + 1) * 512],
                        yb.ap[:, :], b.s_gnw[:, :], rbbf[:, :],
                        ALU.mult, ALU.mult),
                        waits=[rb, cl] + yb.rd_waits(), inc="dve")
                    yb.add_reader(yfin)
                    self._rbfree = yfin
                    yg_toks.append(yfin)
                    self._gnq.popleft()

            load_hb(0)
            em.do("sync", lambda e: e.nop())
            for sl8 in range(8):
                emit_qslice(0, sl8)
            self._ropes = {0: emit_rope(0)}
            for c in range(NCH):
                emit_decay(0, c)
            self._hbfree[0] = em.tok("pe")
            if "dcy0" in b.taps:
                em.do("sync", lambda e: e.dma_start(out=b.dT["dcy0"], in_=dcy(0)[:, :]),
                      waits=[self._dk[(0, c)] for c in range(NCH)], inc="s", inc_n=16)
                em.do("sync", lambda e: e.dma_start(out=b.dT["egr0"], in_=egr(0)[:, :]),
                      waits=[self._eg2[(0, c)] for c in range(NCH)], inc="s", inc_n=16)
            load_hb(1)
            for kv in range(KVH):
                kp = kv % 2
                prevrd = getattr(self, "_kvrd", [None, None])[kp] if kv >= 2 else None
                lk = em.do("sync", lambda e, kv=kv, kp=kp: e.dma_start(
                    out=k_kv(kp)[:, :, :],
                    in_=b.k_dram[:, kv * 128:(kv + 1) * 128]
                        .rearrange("(c p) e -> p c e", p=128)),
                    waits=[self._kvfree[kp]] if kv >= 2 else [], inc="l", inc_n=16)
                lv = em.do("sync", lambda e, kv=kv: e.dma_start(
                    out=v_kv(0)[:, :, :],
                    in_=b.v_dram[:, kv * 128:(kv + 1) * 128]
                        .rearrange("(c p) e -> p c e", p=128)),
                    waits=[self._vkvfree], inc="l", inc_n=16)
                lt = em.do("sync", lambda e, kv=kv, kp=kp: e.dma_start(
                    out=kfm(kp)[:, :], in_=b.k_dram[:, kv * 128:(kv + 1) * 128],
                    transpose=True),
                    waits=[], inc="l", inc_n=16)
                for hh in range(4):
                    h = kv * 4 + hh
                    hp = h % 2
                    rope_done = self._ropes.pop(h)
                    if h == 0 and "qfm0" in b.taps:
                        em.do("sync", lambda e: e.dma_start(
                            out=b.dT["qfm0"], in_=qfm(0)[:, :]),
                            waits=rope_done, inc="s", inc_n=16)
                    stt_prev = sin_tok
                    self._xckd[0] = (
                        em.do("vector", lambda e, h=h, kp=kp: e.tensor_scalar(
                            xcb(0)[:, :], v_kv(kp)[:, 0, :],
                            b.s_dttm[:, h: h + 1], None, ALU.mult),
                            waits=[lv, self._xcfree[0]], inc="dve"),
                        em.do("vector", lambda e, h=h, kp=kp: e.tensor_scalar(
                            kdb(0)[:, :], k_kv(kp)[:, 0, :],
                            b.s_eglg[:, h: h + 1], None, ALU.mult),
                            waits=[lk, self._kdfree[0]], inc="dve"))
                    ylast = []
                    import os as _os3
                    _ser = _os3.environ.get("KSER", "") == "h0" and h == 0
                    for c in range(NCH):
                        if _ser:
                            allw = [em.tok(s) for s in ("pe", "dve", "act", "l", "w", "s", "pool")]
                            em.do("tensor", lambda e: e.nop(), waits=allw)
                            em.do("vector", lambda e: e.nop(), waits=allw)
                            em.do("scalar", lambda e: e.nop(), waits=allw)
                        it = c % 2
                        cg = c // 4
                        yb = p_y[cg]
                        smm = em.do("tensor", lambda e, c=c, kp=kp, hp=hp: e.matmul(
                            p_st.ap[:, 0:128], kfm(kp)[:, c * 128:(c + 1) * 128],
                            qfm(hp)[:, c * 128:(c + 1) * 128],
                            start=True, stop=True),
                            waits=[lt, rope_done[c // 4]] + p_st.wr_waits(), inc="pe")
                        p_st.set_writer(smm)
                        mt = em.do("vector", lambda e, it=it, c=c, hp=hp: e.tensor_mul(
                            mtb(it)[:, :], p_st.ap[:, 0:128],
                            dcy(hp)[:, c * 128:(c + 1) * 128]),
                            waits=[self._dk[(h, c)], self._mtfree[it]] + p_st.rd_waits(), inc="dve")
                        p_st.add_reader(mt)
                        if h == 0 and c == 2 and "mtb2" in b.taps:
                            em.do("sync", lambda e, it=it: e.dma_start(
                                out=b.dT["mtb2"], in_=mtb(it)[:, :]),
                                waits=[mt], inc="s", inc_n=16)
                        qg = em.do("vector", lambda e, it=it, c=c, hp=hp: e.tensor_mul(
                            qgb(it)[:, :], qfm(hp)[:, c * 128:(c + 1) * 128],
                            egr(hp)[:, c * 128:(c + 1) * 128]),
                            waits=[self._eg2[(h, c)], rope_done[c // 4], self._qgfree[it]], inc="dve")
                        sbc = em.do("scalar", lambda e, it=it, h=h: e.copy(
                            stb(it)[:, :], b.states[:, h * 129: h * 129 + 128]),
                            waits=[stt_prev, self._stbfree[it]], inc="act")
                        if c + 1 < NCH:
                            it2 = (c + 1) % 2
                            self._xckd[it2] = (
                                em.do("vector", lambda e, it2=it2, c=c, h=h, kp=kp: e.tensor_scalar(
                                    xcb(it2)[:, :], v_kv(kp)[:, c + 1, :],
                                    b.s_dttm[:, (c + 1) * 32 + h: (c + 1) * 32 + h + 1], None, ALU.mult),
                                    waits=[lv, self._xcfree[it2]], inc="dve"),
                                em.do("vector", lambda e, it2=it2, c=c, h=h, kp=kp: e.tensor_scalar(
                                    kdb(it2)[:, :], k_kv(kp)[:, c + 1, :],
                                    b.s_eglg[:, (c + 1) * 32 + h: (c + 1) * 32 + h + 1], None, ALU.mult),
                                    waits=[lk, self._kdfree[it2]], inc="dve"))
                        xc, kd = self._xckd[it]
                        if h + 1 < H:
                            emit_qslice(h + 1, c)
                            emit_decay(h + 1, c)
                        ym1 = em.do("tensor", lambda e, c=c, it=it, yb=yb: e.matmul(
                            yb.ap[:, (c % 4) * 128:(c % 4 + 1) * 128],
                            xcb(it)[:, :], mtb(it)[:, :],
                            start=True, stop=False),
                            waits=[xc, mt] + (yb.wr_waits() if c % 4 == 0 else []), inc="pe")
                        ym2 = em.do("tensor", lambda e, c=c, it=it, yb=yb: e.matmul(
                            yb.ap[:, (c % 4) * 128:(c % 4 + 1) * 128],
                            stb(it)[:, :], qgb(it)[:, :],
                            start=False, stop=True),
                            waits=[sbc, qg], inc="pe")
                        if c % 4 == 0:
                            yb.set_writer(ym2)
                        else:
                            yb.writer = merge_tok(yb.writer, ym2)
                        cmm = em.do("tensor", lambda e, it=it: e.matmul(
                            p_cs.ap[:, 0:128], kdb(it)[:, :], xcb(it)[:, :],
                            start=True, stop=True),
                            waits=[kd, xc] + p_cs.wr_waits(), inc="pe")
                        p_cs.set_writer(cmm)
                        stt = em.do("vector", lambda e, c=c, h=h: e.scalar_tensor_tensor(
                            b.states[:, h * 129: h * 129 + 128],
                            b.states[:, h * 129: h * 129 + 128],
                            b.s_egl[:, c * 32 + h: c * 32 + h + 1],
                            p_cs.ap[:, 0:128], ALU.mult, ALU.add),
                            waits=[cmm, sbc] + p_cs.rd_waits(), inc="dve")
                        p_cs.add_reader(stt)
                        if h == 0 and "st0" in b.taps:
                            em.do("sync", lambda e, c=c: e.dma_start(
                                out=b.dT["st0"][:, c * 128:(c + 1) * 128],
                                in_=b.states[:, 0:128]),
                                waits=[stt], inc="s", inc_n=16)
                        stt_prev = stt
                        self._mtfree[it] = ym1
                        self._qgfree[it] = ym2
                        self._xcfree[it] = merge_tok(ym1, cmm)
                        self._kdfree[it] = cmm
                        self._stbfree[it] = ym2
                        if c % 4 == 3:
                            self._gnq.append(dict(st=0, h=h, cg=cg, yb=yb))
                        gn_pump()
                    if h + 1 < H:
                        self._ropes[h + 1] = emit_rope(h + 1)
                    if h + 2 < H:
                        load_hb(h + 2)
                    self._dcyfree[hp] = em.tok("dve")
                    self._egrfree[hp] = em.tok("dve")
                    self._qfree[hp] = em.tok("pe")
                    self._hbfree[(hp + 1) % 2] = em.tok("pe")
                self._kvfree[kp] = em.tok("dve")
                self._vkvfree = merge_tok(em.tok("dve"), em.tok("pe"))
            while self._gnq:
                gn_pump()
            b.yB_done = em.tok("dve")
            if "yall" in b.taps:
                em.do("sync", lambda e: e.dma_start(out=b.dT["yall"], in_=yall[:, :]),
                      waits=yg_toks[-4:] + [em.tok("dve")], inc="s", inc_n=16)
            em.do("sync", lambda e: e.nop(),
                  waits=[em.tok("s"), em.tok("l"), em.tok("w")])
        self._qrope = None
        self._qfree = [None, None]
        self._hbfree = [None, None]
        self._kvfree = [None, None]
        self._dmfree = [None, None]
        self._dcyfree = [None, None]
        self._egrfree = [None, None]
        self._mtfree = [None, None]
        self._qgfree = [None, None]
        self._xcfree = [None, None]
        self._kdfree = [None, None]
        self._stbfree = [None, None]
        self._y2free = None
        self._rsfree = None
        self._rsbffree = None
        self._rbfree = None

        def reset_and_run(em):
            import collections as _coll
            self._gnq = _coll.deque()
            self._xckd = {0: None, 1: None}
            self._qmm = {0: None, 1: None}
            self._ropes = {}
            self._dk = {}
            self._eg2 = {}
            self._hbt = [None, None]
            self._vkvfree = None
            self._qrope = None
            self._qfree = [None, None]
            self._hbfree = [None, None]
            self._kvfree = [None, None]
            self._dmfree = [None, None]
            self._dcyfree = [None, None]
            self._egrfree = [None, None]
            self._mtfree = [None, None]
            self._qgfree = [None, None]
            self._xcfree = [None, None]
            self._kdfree = [None, None]
            self._stbfree = [None, None]
            self._y2free = None
            self._rsfree = None
            self._rsbffree = None
            self._rbfree = None
            body(em)
        self.run_block(reset_and_run)
        st.close()

    def phaseC(self, yall):
        b = self
        st = contextlib.ExitStack()
        sb = lambda n, shp, dt: self.sb(n, shp, dt, st)
        wgb_ = sb("wgb", [128, 2, DM], BF16)
        wgb = lambda i: wgb_[:, i]
        silu_bf = sb("silu_bf", [128, 512], BF16)
        p_q = [self.ps(f"pC_q{i}", st) for i in range(2)]

        def body(em):
            yprev = b.yB_done
            qctr = [0]
            wqtok = {}
            wqcons = {0: None, 1: None}

            def wg_load():
                h = qctr[0]
                qctr[0] += 1
                slot = h % 2
                wqtok[h] = em.do("gpsimd", lambda e, h=h, slot=slot: e.dma_start(
                    out=wgb(slot)[:, :], in_=b.wg[h]),
                    waits=[wqcons[slot]], inc="w", inc_n=16)
            wg_load()
            wg_load()
            prev_silu = None
            for h in range(H):
                qmm = {0: None, 1: None}
                slot = h % 2
                for d in range(NDC):
                    for ts in range(2):
                        bank = p_q[ts]
                        w0 = [wqtok[h]]
                        if d == 0:
                            w0 += bank.wr_waits()
                        qmm[ts] = em.do("tensor", lambda e, d=d, ts=ts, slot=slot, bank=bank: e.matmul(
                            bank.ap[:, :],
                            wgb(slot)[:, d * 128:(d + 1) * 128],
                            b.xfm[:, d * TPC + ts * 512: d * TPC + (ts + 1) * 512],
                            start=(d == 0), stop=(d == NDC - 1)),
                            waits=w0, inc="pe")
                wqcons[slot] = qmm[1]
                if qctr[0] < H:
                    wg_load()
                for ts in range(2):
                    bank = p_q[ts]
                    bank.set_writer(qmm[ts])
                    sl = em.do("scalar", lambda e, bank=bank: e.activation(
                        silu_bf[:, :], bank.ap[:, :], AF.Silu),
                        waits=bank.rd_waits() + [prev_silu], inc="act")
                    bank.add_reader(sl)
                    g0t = None
                    if h == 0 and "gate0" in b.taps:
                        g0t = em.do("sync", lambda e, ts=ts: e.dma_start(
                            out=b.dT["gate0"][:, ts * 512:(ts + 1) * 512], in_=silu_bf[:, :]),
                            waits=[sl], inc="s", inc_n=16)
                    gm = em.do("vector", lambda e, h=h, ts=ts: e.tensor_mul(
                        yall[:, h * TPC + ts * 512: h * TPC + (ts + 1) * 512],
                        yall[:, h * TPC + ts * 512: h * TPC + (ts + 1) * 512],
                        silu_bf[:, :]),
                        waits=[sl, yprev], inc="dve")
                    prev_silu = merge_tok(gm, g0t)
            b.yC_done = em.tok("dve")
            if "yallC" in b.taps:
                em.do("sync", lambda e: e.dma_start(out=b.dT["yallC"], in_=yall[:, :]),
                      waits=[b.yC_done], inc="s", inc_n=16)
            em.do("sync", lambda e: e.nop(), waits=[em.tok("s"), em.tok("w")])
        self.run_block(body)
        st.close()

    def phaseD(self, yall):
        b = self
        st = contextlib.ExitStack()
        sb = lambda n, shp, dt: self.sb(n, shp, dt, st)
        wob_ = sb("wob", [128, 2, 16 * 512], BF16)
        wob = lambda i: wob_[:, i]
        stage_ = sb("stageD", [128, 2, 512], F32)
        stage = lambda i: stage_[:, i]
        p_o = [self.ps(f"pD_{i}", st) for i in range(8)]

        def body(em):
            yc = b.yC_done
            wtok = {}
            wcons = {0: None, 1: None}
            nblk = 16

            def wload(i):
                o, hh = divmod(i, 2)
                slot = i % 2
                wtok[i] = em.do("gpsimd", lambda e, o=o, hh=hh, slot=slot: e.dma_start(
                    out=wob(slot)[:, :],
                    in_=b.wo[o, :, hh * 16 * 512:(hh + 1) * 16 * 512]),
                    waits=[wcons[slot]], inc="w", inc_n=16)
            wload(0)
            wload(1)
            stfree = [None, None]
            for o in range(8):
                mmlast = {}
                for hh in range(2):
                    i = o * 2 + hh
                    slot = i % 2
                    for h16 in range(16):
                        h = hh * 16 + h16
                        for c in range(8):
                            bank = p_o[c]
                            w0 = [wtok[i], yc]
                            if h == 0:
                                w0 += bank.wr_waits()
                            mmlast[c] = em.do("tensor", lambda e, h=h, h16=h16, c=c, slot=slot, bank=bank: e.matmul(
                                bank.ap[:, :],
                                yall[:, h * TPC + c * 128: h * TPC + (c + 1) * 128],
                                wob(slot)[:, h16 * 512:(h16 + 1) * 512],
                                start=(h == 0), stop=(h == H - 1)),
                                waits=w0, inc="pe")
                    wcons[slot] = mmlast[7]
                    if i + 2 < nblk:
                        wload(i + 2)
                for c in range(8):
                    bank = p_o[c]
                    bank.set_writer(mmlast[c])
                    sp = c % 2
                    ev = em.do("scalar", lambda e, c=c, sp=sp, bank=bank: e.copy(
                        stage(sp)[:, :], bank.ap[:, :]),
                        waits=bank.rd_waits() + [stfree[sp]], inc="act")
                    bank.add_reader(ev)
                    sdma = em.do("sync", lambda e, c=c, o=o, sp=sp: e.dma_start(
                        out=b.attn_dram[c * 128:(c + 1) * 128, o * 512:(o + 1) * 512],
                        in_=stage(sp)[:, :]),
                        waits=[ev], inc="s", inc_n=16)
                    stfree[sp] = sdma
            b.attn_done = em.tok("s")
            if "attn" in b.taps:
                em.do("sync", lambda e: e.dma_start(out=b.dT["attn"], in_=b.attn_dram[:, :]),
                      waits=[b.attn_done], inc="s", inc_n=16)
            em.do("sync", lambda e: e.nop(), waits=[em.tok("s"), em.tok("w")])
        self.run_block(body)
        st.close()

    def _emit_ln_rows(self, em, st_bufs, t, in_a, in_b, wait_in, out_dram,
                      bf_dram=None, prev=None):
        b = self
        arow_, xrow_, cols_, x1bf_ = st_bufs
        sp = t % 2
        arow = arow_[:, sp]
        xrow = xrow_[:, sp]
        cols = cols_[:, sp]
        x1bf = x1bf_[:, sp] if x1bf_ is not None else None
        la = em.do("sync", lambda e: e.dma_start(
            out=arow[:, :], in_=in_a[t * 128:(t + 1) * 128, :]),
            waits=[wait_in] + ([prev["arow_free"]] if prev else []), inc="l", inc_n=16)
        lb = em.do("gpsimd", lambda e: e.dma_start(
            out=xrow[:, :], in_=in_b[t * 128:(t + 1) * 128, :]),
            waits=[wait_in] + ([prev["xrow_free"]] if prev else []), inc="w", inc_n=16)
        ad = em.do("vector", lambda e: e.tensor_add(arow[:, :], arow[:, :], xrow[:, :]),
                   waits=[la, lb], inc="dve")
        sq = em.do("scalar", lambda e: e.activation(
            xrow[:, :], arow[:, :], AF.Square, accum_out=cols[:, 0:1]),
            waits=[ad] + ([prev["sq_free"]] if prev else []), inc="act")
        sm = em.do("vector", lambda e: e.reduce_sum(
            cols[:, 1:2], arow[:, :], axis=mybir.AxisListType.X),
            waits=[ad] + ([prev["sm_free"]] if prev else []), inc="dve")
        m1 = em.do("vector", lambda e: e.tensor_scalar(
            cols[:, 2:3], cols[:, 1:2], 1.0 / DM, None, ALU.mult),
            waits=[sm], inc="dve")
        vv = em.do("vector", lambda e: e.tensor_mul(
            cols[:, 3:4], cols[:, 2:3], cols[:, 2:3]), waits=[m1], inc="dve")
        v1 = em.do("vector", lambda e: e.tensor_scalar(
            cols[:, 4:5], cols[:, 0:1], 1.0 / DM, None, ALU.mult),
            waits=[sq], inc="dve")
        v2 = em.do("vector", lambda e: e.tensor_sub(
            cols[:, 5:6], cols[:, 4:5], cols[:, 3:4]),
            waits=[v1, vv], inc="dve")
        r1 = em.do("scalar", lambda e: e.activation(
            cols[:, 6:7], cols[:, 5:6], AF.Ln, bias=b.s_epsB[:, :]),
            waits=[v2], inc="act")
        r2 = em.do("scalar", lambda e: e.activation(
            cols[:, 7:8], cols[:, 6:7], AF.Exp, scale=-0.5),
            waits=[r1], inc="act")
        xn = em.do("vector", lambda e: e.tensor_scalar(
            xrow[:, :], arow[:, :], cols[:, 2:3], cols[:, 7:8],
            ALU.subtract, ALU.mult), waits=[r2, sq], inc="dve")
        so = em.do("scalar", lambda e: e.dma_start(
            out=out_dram[t * 128:(t + 1) * 128, :], in_=xrow[:, :]),
            waits=[xn], inc="s2", inc_n=16)
        toks = dict(arow_free=xn, sq_free=xn, sm_free=m1, out=so)
        if bf_dram is not None:
            cb = em.do("vector", lambda e: e.tensor_copy(x1bf[:, :], xrow[:, :]),
                       waits=[xn] + ([prev["bf_free"]] if prev else []), inc="dve")
            sb2 = em.do("scalar", lambda e: e.dma_start(
                out=bf_dram[t * 128:(t + 1) * 128, :], in_=x1bf[:, :]),
                waits=[cb], inc="s2", inc_n=16)
            toks["bf_free"] = sb2
            toks["bfout"] = sb2
            toks["xrow_free"] = merge_tok(cb, so)
        else:
            toks["xrow_free"] = so
        return toks

    def phaseE(self):
        b = self
        st = contextlib.ExitStack()
        sb = lambda n, shp, dt: self.sb(n, shp, dt, st)
        arow = sb("arowE", [128, 2, DM], F32)
        xrow = sb("xrowE", [128, 2, DM], F32)
        cols = sb("colsE", [128, 2, 8], F32)
        x1bf = sb("x1bfE", [128, 2, DM], BF16)
        b.s_epsB = getattr(b, "s_epsB", None)

        def body(em):
            if b.s_epsB is None:
                b.s_epsB = b._s_eps2
            prevs = {0: None, 1: None}
            bfouts = []
            for t in range(NCH):
                prevs[t % 2] = self._emit_ln_rows(em, (arow, xrow, cols, x1bf), t,
                                          b.attn_dram, b.x_tm, b.attn_done,
                                          b.x1_dram, b.x1b_dram, prevs[t % 2])
                bfouts.append(prevs[t % 2]["bfout"])
            tts = []
            for d in range(NDC):
                tts.append(em.do("sync", lambda e, d=d: e.dma_start(
                    out=b.xfm[:, d * TPC:(d + 1) * TPC],
                    in_=b.x1b_dram[:, d * 128:(d + 1) * 128], transpose=True),
                    waits=bfouts, inc="l", inc_n=16))
            b.x1fm_done = merge_tok(*tts)
            if "xfmE" in b.taps:
                em.do("sync", lambda e: e.dma_start(out=b.dT["xfmE"], in_=b.xfm[:, :]),
                      waits=[b.x1fm_done], inc="s", inc_n=16)
            b.x1_done = merge_tok(em.tok("s"), em.tok("s2"), em.tok("w"))
            if "x1" in b.taps:
                em.do("sync", lambda e: e.dma_start(out=b.dT["x1"], in_=b.x1_dram[:, :]),
                      waits=[b.x1_done], inc="s", inc_n=16)
            em.do("sync", lambda e: e.nop(),
                  waits=[em.tok("s"), em.tok("s2"), em.tok("l"), em.tok("w")])
        self.run_block(body)
        st.close()

    def phaseF(self):
        b = self
        st = contextlib.ExitStack()
        sb = lambda n, shp, dt: self.sb(n, shp, dt, st)
        wab = sb("wab", [128, NDC * 128], BF16)
        h_bf = sb("h_bf", [128, TPC], BF16)
        bcat = sb("bcat", [128, 8 * 512], BF16)
        fbrow = sb("fbrow", [1, DM], BF16)
        s_orB = sb("s_orBF", [1, 512], BF16)
        wfb_ = sb("wfb", [128, 2, 16 * 512], BF16)
        wfb = lambda i: wfb_[:, i]
        stage_ = sb("stageF", [128, 2, 512], F32)
        stage = lambda i: stage_[:, i]
        p_o = [self.ps(f"pF_{i}", st) for i in range(8)]

        def body(em):
            x1t = b.x1fm_done
            wa_t = em.do("gpsimd", lambda e: e.dma_start(out=wab[:, :], in_=b.wa),
                         inc="w", inc_n=16)
            bc_t = em.do("gpsimd", lambda e: e.dma_start(
                out=bcat[:, :].rearrange("p (o e) -> p o e", o=8),
                in_=b.wb.rearrange("o p e -> p o e")),
                inc="w", inc_n=16)
            fb_t = em.do("gpsimd", lambda e: e.dma_start(out=fbrow[:, :], in_=b.fbias),
                         inc="w", inc_n=16)
            or_t = em.do("gpsimd", lambda e: e.dma_start(out=s_orB[:, :], in_=b.onesrow),
                         inc="w", inc_n=16)
            hdone = []
            for ts in range(2):
                bank = p_o[ts]
                mm = None
                for d in range(NDC):
                    mm = em.do("tensor", lambda e, d=d, ts=ts, bank=bank: e.matmul(
                        bank.ap[:, :], wab[:, d * 128:(d + 1) * 128],
                        b.xfm[:, d * TPC + ts * 512: d * TPC + (ts + 1) * 512],
                        start=(d == 0), stop=(d == NDC - 1)),
                        waits=[wa_t, x1t] + (bank.wr_waits() if d == 0 else []),
                        inc="pe")
                bank.set_writer(mm)
                hv = em.do("vector", lambda e, ts=ts, bank=bank: e.tensor_scalar(
                    h_bf[:, ts * 512:(ts + 1) * 512], bank.ap[:, :],
                    b.s_lwB[:, :], 2.0, ALU.mult, ALU.mult),
                    waits=bank.rd_waits(), inc="dve")
                bank.add_reader(hv)
                hdone.append(hv)
            wtok = {}
            wcons = {0: None, 1: None}

            def wload(i):
                o, dh = divmod(i, 2)
                slot = i % 2
                wtok[i] = em.do("gpsimd", lambda e, o=o, dh=dh, slot=slot: e.dma_start(
                    out=wfb(slot)[:, :],
                    in_=b.wf[o, :, dh * 16 * 512:(dh + 1) * 16 * 512]),
                    waits=[wcons[slot]], inc="w", inc_n=16)
            wload(0)
            wload(1)
            stfree = [None, None]
            for o in range(8):
                mmlast = {}
                for dh in range(2):
                    i = o * 2 + dh
                    slot = i % 2
                    for d16 in range(16):
                        d = dh * 16 + d16
                        for c in range(8):
                            bank = p_o[c]
                            w0 = [wtok[i], x1t]
                            if d == 0:
                                w0 += bank.wr_waits()
                            mmlast[c] = em.do("tensor", lambda e, d=d, d16=d16, c=c, slot=slot, bank=bank: e.matmul(
                                bank.ap[:, :],
                                b.xfm[:, d * TPC + c * 128: d * TPC + (c + 1) * 128],
                                wfb(slot)[:, d16 * 512:(d16 + 1) * 512],
                                start=(d == 0), stop=False),
                                waits=w0, inc="pe")
                    wcons[slot] = mmlast[7]
                    if i + 2 < 16:
                        wload(i + 2)
                for c in range(8):
                    bank = p_o[c]
                    lm = em.do("tensor", lambda e, c=c, o=o, bank=bank: e.matmul(
                        bank.ap[:, :], h_bf[:, c * 128:(c + 1) * 128],
                        bcat[:, o * 512:(o + 1) * 512],
                        start=False, stop=False),
                        waits=[bc_t] + hdone, inc="pe")
                    bm = em.do("tensor", lambda e, c=c, o=o, bank=bank: e.matmul(
                        bank.ap[:, :], s_orB[0:1, 0:128],
                        fbrow[0:1, o * 512:(o + 1) * 512],
                        start=False, stop=True),
                        waits=[fb_t, or_t], inc="pe")
                    bank.set_writer(bm)
                    sp = c % 2
                    ev = em.do("scalar", lambda e, c=c, sp=sp, bank=bank: e.copy(
                        stage(sp)[:, :], bank.ap[:, :]),
                        waits=bank.rd_waits() + [stfree[sp]], inc="act")
                    bank.add_reader(ev)
                    sdma = em.do("sync", lambda e, c=c, o=o, sp=sp: e.dma_start(
                        out=b.ffn_dram[c * 128:(c + 1) * 128, o * 512:(o + 1) * 512],
                        in_=stage(sp)[:, :]),
                        waits=[ev], inc="s", inc_n=16)
                    stfree[sp] = sdma
            b.ffn_done = em.tok("s")
            if "ffn" in b.taps:
                em.do("sync", lambda e: e.dma_start(out=b.dT["ffn"], in_=b.ffn_dram[:, :]),
                      waits=[b.ffn_done], inc="s", inc_n=16)
            em.do("sync", lambda e: e.nop(), waits=[em.tok("s"), em.tok("w"), em.tok("l")])
        self.run_block(body)
        st.close()

    def phaseG(self):
        b = self
        st = contextlib.ExitStack()
        sb = lambda n, shp, dt: self.sb(n, shp, dt, st)
        arow = sb("arowG", [128, 2, DM], F32)
        xrow = sb("xrowG", [128, 2, DM], F32)
        cols = sb("colsG", [128, 2, 8], F32)

        def body(em):
            prevs = {0: None, 1: None}
            outs = []
            for t in range(NCH):
                prevs[t % 2] = self._emit_ln_rows(em, (arow, xrow, cols, None), t,
                                          b.ffn_dram, b.x1_dram, b.ffn_done,
                                          b.out_t, None, prevs[t % 2])
                outs.append(prevs[t % 2]["out"])
            em.do("sync", lambda e: e.nop(),
                  waits=[em.tok("s"), em.tok("s2"), em.tok("l"), em.tok("w")])
        self.run_block(body)
        st.close()

    def finish_stub(self):
        b = self
        st = contextlib.ExitStack()
        zb = self.sb("zstub", [128, DM], F32, st)

        def body(em):
            z = em.do("gpsimd", lambda e: e.memset(zb[:, :], 0.0), inc="pool")
            for t in range(NCH):
                em.do("sync", lambda e, t=t: e.dma_start(
                    out=b.out_t[t * 128:(t + 1) * 128, :], in_=zb[:, :]),
                    waits=[z], inc="s", inc_n=16)
            em.do("sync", lambda e: e.nop(), waits=[em.tok("s")])
        self.run_block(body)
        st.close()


def build_program(taps=()):
    b = Builder(taps)
    with b.es:
        b.declare()
        b.phase0()
        import os as _os
        lvl = int(_os.environ.get('KLEVEL', '9'))
        b.phaseA(level=min(lvl, 4))
        if lvl >= 5:
            b.yall_stack = contextlib.ExitStack()
            yall = b.sb("yall", [128, H * TPC], BF16, b.yall_stack)
            b.phaseB(yall)
            if lvl >= 6:
                b.phaseC(yall)
            if lvl >= 7:
                b.phaseD(yall)
            b.yall_stack.close()
            if lvl >= 8:
                b.phaseE()
            if lvl >= 9:
                b.phaseF()
                b.phaseG()
        if lvl < 9:
            b.finish_stub()
    return b


def pack_weights(ins):
    Wq = ins["L0_attn_Wq"]; Wk = ins["L0_attn_Wk"]; Wv = ins["L0_attn_Wv"]
    Wo = ins["L0_attn_Wo"]; Wg = ins["L0_attn_Wg"]; Win = ins["L0_attn_Win"]
    Wf = ins["L0_ffn_W"]; lA = ins["L0_ffn_lora_A"]; lB = ins["L0_ffn_lora_B"]
    P = {}
    for nm, W in (("wq", Wq), ("wg", Wg)):
        P[nm] = np.ascontiguousarray(
            W.reshape(H, 128, NDC, 128).transpose(0, 3, 2, 1).reshape(H, 128, DM))
    for nm, W in (("wk", Wk), ("wv", Wv)):
        P[nm] = np.ascontiguousarray(
            W.reshape(2, 512, NDC, 128).transpose(0, 3, 2, 1).reshape(2, 128, NDC * 512))
    P["wo"] = np.ascontiguousarray(
        Wo.reshape(8, 512, H, 128).transpose(0, 3, 2, 1).reshape(8, 128, H * 512))
    P["wf"] = np.ascontiguousarray(
        Wf.reshape(8, 512, NDC, 128).transpose(0, 3, 2, 1).reshape(8, 128, NDC * 512))
    P["win"] = np.ascontiguousarray(
        Win.reshape(32, NDC, 128).transpose(2, 1, 0).reshape(128, NDC * 32))
    A_all = lA.reshape(128, DM)
    P["wa"] = np.ascontiguousarray(
        A_all.reshape(128, NDC, 128).transpose(2, 1, 0).reshape(128, NDC * 128))
    Bcat = lB.transpose(0, 2, 1).reshape(128, DM)
    P["wb"] = np.ascontiguousarray(Bcat.reshape(128, 8, 512).transpose(1, 0, 2))
    P["fbias"] = np.ascontiguousarray(ins["L0_ffn_bias"].reshape(1, DM))
    P["dtb"] = np.ascontiguousarray(ins["L0_attn_dt_bias"].reshape(32, 1))
    P["alog"] = np.ascontiguousarray(ins["L0_attn_A_log_bias"].reshape(32, 1))
    P["gnw"] = np.ascontiguousarray(ins["L0_attn_gnorm_w"].reshape(128, 1))
    P["lwc"] = np.ascontiguousarray(
        np.repeat(ins["L0_ffn_lora_weights"], 16).reshape(128, 1))
    return P


def host_constants():
    C = {}
    C["identc"] = np.eye(128, dtype=BF)
    C["identf"] = np.eye(32, dtype=np.float32)
    idx = np.arange(128)
    mask_ui = (idx[None, :] >= idx[:, None])
    C["negmaskc"] = np.where(mask_ui, np.float32(0), np.float32(-3e38)).astype(np.float32)
    hb = np.zeros((32, H * 128), np.float32)
    for h in range(H):
        hb[h, h * 128:(h + 1) * 128] = 1.0
    C["hbc"] = hb
    C["ones32c"] = np.ones((32, 128), np.float32)
    C["onesrow"] = np.ones((1, 512), np.float32)
    C["onescol_bf"] = np.ones((128, 1), dtype=BF)
    C["epscol"] = np.full((128, 1), 1e-5, dtype=np.float32)
    C["onesrow_bf"] = np.ones((1, 128), dtype=BF)
    return C


def rope_tables(r0):
    pos = np.arange(r0, r0 + TPC, dtype=np.float32)
    inv = 1.0 / 10000.0 ** (np.arange(0, D, 2, dtype=np.float32) / D)
    ang = pos[:, None] * inv[None, :]
    cos = np.concatenate([np.cos(ang), np.cos(ang)], -1).astype(np.float32)
    sin = np.concatenate([np.sin(ang), np.sin(ang)], -1).astype(np.float32)
    return cos, sin


def make_in_maps(inputs):
    P = pack_weights(inputs)
    C = host_constants()
    x = inputs["L0_attn_hidden_states"]
    maps = []
    for core in range(NCORES):
        bb, p = core // 4, core % 4
        r0 = p * TPC
        m = dict(P)
        m.update(C)
        xs = x[bb, r0:r0 + TPC, :]
        m["x_tm"] = np.ascontiguousarray(xs)
        m["x_fm"] = np.ascontiguousarray(
            xs.T.reshape(4, 8, 128, TPC).transpose(0, 2, 1, 3).reshape(4, 128, 8 * TPC))
        cos, sin = rope_tables(r0)
        m["cos_tm"] = cos.astype(BF)
        m["sin_tm"] = sin.astype(BF)
        m["cosT"] = np.ascontiguousarray(cos.T).astype(BF)
        m["sinT"] = np.ascontiguousarray(sin.T).astype(BF)
        mc = np.zeros((128, 4), np.float32)
        mc[:, :p] = 1.0
        m["mcol"] = mc
        m["invmcol"] = np.ascontiguousarray(1.0 - mc)
        maps.append(m)
    return maps


def kernel(**inputs):
    b = build_program()
    in_maps = make_in_maps(inputs)
    res = run_bass_kernel_spmd(b.nc, in_maps, core_ids=list(range(NCORES)))
    out = np.empty((2, 4096, 4096), np.float32)
    for c in range(NCORES):
        out[c // 4, (c % 4) * TPC:(c % 4 + 1) * TPC, :] = res.results[c]["out"]
    return out


# version 21 (final)
# speedup vs baseline: 1.0819x; 1.0819x over previous
import contextlib

import numpy as np
import ml_dtypes

import concourse.bass as bass
import concourse.mybir as mybir
from concourse.bass_utils import run_bass_kernel_spmd

F32 = mybir.dt.float32
BF16 = mybir.dt.bfloat16
AF = mybir.ActivationFunctionType
ALU = mybir.AluOpType
BF = ml_dtypes.bfloat16

H = 32
KVH = 8
D = 128
CH = 128
TPC = 1024
NCH = TPC // CH
DM = 4096
NDC = DM // 128
NCORES = 8
EPS = 1e-5

ENGINES = ("tensor", "vector", "scalar", "gpsimd", "sync")
SEM_OWNER = {"pe": "tensor"}


class Emitter:

    def __init__(self, nc, sems):
        self.nc = nc
        self.sems = sems
        self.base = {k: 0 for k in sems}
        self.cnt = dict(self.base)
        self.cur = None
        self.eng = None
        self.lastwait = {}

    def start_pass(self, name, eng):
        self.cur = name
        self.eng = eng
        self.cnt = dict(self.base)
        self.lastwait = {}

    def commit(self):
        self.base = dict(self.cnt)

    def do(self, engine, fn, waits=(), inc=None, inc_n=1):
        if self.cur == engine:
            merged = {}
            for t in waits:
                if t is None:
                    continue
                for s, v in t.items():
                    if v > merged.get(s, 0):
                        merged[s] = v
            for s, v in sorted(merged.items()):
                if v > self.cnt[s]:
                    raise RuntimeError(
                        f"{engine} waits {s}>={v} > modeled {self.cnt[s]}")
                if SEM_OWNER.get(s) == engine:
                    continue
                if self.lastwait.get(s, -1) < v:
                    self.eng.wait_ge(self.sems[s], v)
                    self.lastwait[s] = v
            inst = fn(self.eng)
            if inc is not None:
                assert inst is not None
                inst.then_inc(self.sems[inc], inc_n)
        if inc is not None:
            self.cnt[inc] += inc_n
            return {inc: self.cnt[inc]}
        return {}

    def tok(self, sem):
        return {sem: self.cnt[sem]}


def merge_tok(*toks):
    m = {}
    for t in toks:
        if not t:
            continue
        for s, v in t.items():
            if v > m.get(s, 0):
                m[s] = v
    return m


class Bank:

    def __init__(self, ap):
        self.ap = ap
        self.writer = None
        self.readers = []

    def wr_waits(self):
        w = list(self.readers)
        if self.writer:
            w.append(self.writer)
        return w

    def set_writer(self, tok):
        self.writer = tok
        self.readers = []

    def rd_waits(self):
        return [self.writer] if self.writer else []

    def add_reader(self, tok):
        self.readers.append(tok)


class Builder:
    def __init__(self, taps=()):
        self.nc = bass.Bass("TRN2", target_bir_lowering=False)
        self.taps = set(taps)
        self.dT = {}
        self.es = contextlib.ExitStack()

    def din(self, name, shape, dt=F32):
        return self.nc.dram_tensor(name, list(shape), dt, kind="ExternalInput").ap()

    def dint(self, name, shape, dt=F32):
        return self.nc.dram_tensor(name, list(shape), dt)

    def tap(self, name, shape, dt=F32):
        if name in self.taps:
            self.dT[name] = self.nc.dram_tensor(
                "t_" + name, list(shape), dt, kind="ExternalOutput").ap()

    def sb(self, name, shape, dt, stack=None):
        return (stack or self.es).enter_context(
            self.nc.sbuf_tensor(name, list(shape), dt))

    def ps(self, name, stack=None):
        return Bank((stack or self.es).enter_context(
            self.nc.psum_tensor(name, [128, 512], F32)))

    def load_const(self, em, dst, src):
        return em.do("sync", lambda e: e.dma_start(out=dst, in_=src),
                     inc="l", inc_n=16)

    def run_block(self, body):
        with self.nc.Block() as blk:
            for name in ENGINES:
                def make(nm):
                    def f(eng):
                        self.em.start_pass(nm, eng)
                        body(self.em)
                    return f
                getattr(blk, name)(make(name))
        self.em.commit()

    def declare(self):
        nc = self.nc
        self.x_fm = self.din("x_fm", [4, 128, 8 * TPC])
        self.x_tm = self.din("x_tm", [TPC, DM])
        self.wq = self.din("wq", [H, 128, DM])
        self.wg = self.din("wg", [H, 128, DM])
        self.wk = self.din("wk", [2, 128, NDC * 512])
        self.wv = self.din("wv", [2, 128, NDC * 512])
        self.wo = self.din("wo", [8, 128, H * 512])
        self.wf = self.din("wf", [8, 128, NDC * 512])
        self.win = self.din("win", [128, NDC * 32])
        self.wa = self.din("wa", [128, NDC * 128])
        self.wb = self.din("wb", [8, 128, 512])
        self.fbias = self.din("fbias", [1, DM])
        self.dtb = self.din("dtb", [32, 1])
        self.alog = self.din("alog", [32, 1])
        self.gnw = self.din("gnw", [128, 1])
        self.lwc = self.din("lwc", [128, 1])
        self.cos_tm = self.din("cos_tm", [TPC, 128], BF16)
        self.sin_tm = self.din("sin_tm", [TPC, 128], BF16)
        self.cosT = self.din("cosT", [128, TPC], BF16)
        self.sinT = self.din("sinT", [128, TPC], BF16)
        self.identc = self.din("identc", [128, 128], BF16)
        self.identf = self.din("identf", [32, 32])
        self.negmaskc = self.din("negmaskc", [128, 128])
        self.hbc = self.din("hbc", [32, H * 128])
        self.ones32c = self.din("ones32c", [32, 128])
        self.onesrow = self.din("onesrow", [1, 512])
        self.onescol_bf = self.din("onescol_bf", [128, 1], BF16)
        self.epscol = self.din("epscol", [128, 1])
        self.onesrow_bf = self.din("onesrow_bf", [1, 128], BF16)
        self.mcol = self.din("mcol", [128, 4])
        self.invmcol = self.din("invmcol", [128, 4])
        self.out_t = nc.dram_tensor("out", [TPC, DM], F32, kind="ExternalOutput").ap()
        self.k_dram = self.dint("k_dram", [TPC, KVH * 128], BF16)
        self.v_dram = self.dint("v_dram", [TPC, KVH * 128], BF16)
        self.attn_dram = self.dint("attn_dram", [TPC, DM])
        self.x1_dram = self.dint("x1_dram", [TPC, DM])
        self.x1b_dram = self.dint("x1b_dram", [TPC, DM], BF16)
        self.ffn_dram = self.dint("ffn_dram", [TPC, DM])
        self.cc_in = self.dint("cc_in", [128, H * 129])
        self.cc_out = self.dint("cc_out", [4 * 128, H * 129])

        sb = self.sb
        self.xfm = sb("xfm", [128, NDC * TPC], BF16)
        self.states = sb("states", [128, H * 129], F32)
        self.s_gT = sb("s_gT", [32, TPC], F32)
        self.s_ngtm = sb("s_ngtm", [128, NCH * 32], F32)
        self.s_eglg = sb("s_eglg", [128, NCH * 32], F32)
        self.s_egl = sb("s_egl", [128, NCH * 32], F32)
        self.s_dttm = sb("s_dttm", [128, NCH * 32], F32)
        self.s_negm = sb("s_negm", [128, 128], F32)
        self.s_ident = sb("s_ident", [128, 128], BF16)
        self.s_gnw = sb("s_gnw", [128, 1], F32)
        self._s_eps2 = sb("s_eps2", [128, 1], F32)
        self.s_mP = sb("s_mP", [128, 4], F32)
        self.s_imP = sb("s_imP", [128, 4], F32)
        self.s_lwB = sb("s_lwB", [128, 1], F32)

        semnames = ["w", "l", "s", "s2", "pe", "dve", "act", "pool", "cc"]
        self.sems = {n: self.es.enter_context(self.nc.semaphore("sm_" + n))
                     for n in semnames}
        self.em = Emitter(self.nc, self.sems)

        self.tap("dtT", [32, TPC])
        self.tap("zb", [32, TPC])
        self.tap("sp1", [32, TPC])
        self.tap("sp2", [32, TPC])
        self.tap("GT", [32, TPC])
        self.tap("payload", [128, H * 129])
        self.tap("sin", [128, H * 129])
        self.tap("kd", [TPC, KVH * 128], BF16)
        self.tap("vd", [TPC, KVH * 128], BF16)
        self.tap("yall", [128, H * TPC], BF16)
        self.tap("yallC", [128, H * TPC], BF16)
        self.tap("gate0", [128, TPC], BF16)
        self.tap("dcy0", [128, NCH * 128])
        self.tap("sinB", [128, H * 129])
        self.tap("st0", [128, NCH * 128])
        self.tap("qfm0", [128, TPC], BF16)
        self.tap("mtb2", [128, 128], BF16)
        self.tap("y2b0", [128, 512], BF16)
        self.tap("rsbf0", [1, 512], BF16)
        self.tap("rbbf0", [128, 512], BF16)
        self.tap("egr0", [128, NCH * 128], BF16)
        self.tap("ffn", [TPC, DM])
        self.tap("xfmE", [128, NDC * TPC], BF16)
        self.tap("attn", [TPC, DM])
        self.tap("x1", [TPC, DM])
        self.tap("gtm", [128, NCH * 32])
        self.tap("eglg", [128, NCH * 32])
        self.tap("egl", [128, NCH * 32])

    def phase0(self):
        b = self

        def body(em):
            for dst, src in [(b.xfm, None)]:
                pass
            b.xw = []
            for i in range(4):
                b.xw.append(em.do("gpsimd", lambda e, i=i: e.dma_start(
                    out=b.xfm[:, i * 8 * TPC:(i + 1) * 8 * TPC],
                    in_=b.x_fm[i]),
                    inc="w", inc_n=16))
            for dst, src in [(b.s_negm[:, :], b.negmaskc), (b.s_ident[:, :], b.identc),
                             (b.s_gnw[:, :], b.gnw), (b._s_eps2[:, :], b.epscol),
                             (b.s_lwB[:, :], b.lwc)]:
                self.load_const(em, dst, src)
            b.pinit = em.do("gpsimd", lambda e: e.memset(b.states[:, :], 0.0), inc="pool")
            b.constok = em.tok("l")
        self.run_block(body)

    def phaseA(self, level=4):
        b = self
        st = contextlib.ExitStack()
        sb = lambda n, shp, dt: self.sb(n, shp, dt, st)
        s_identf = sb("s_identf", [32, 32], F32)
        s_costm = sb("s_costm", [128, NCH, 128], BF16)
        s_sintm = sb("s_sintm", [128, NCH, 128], BF16)
        s_dtb = sb("s_dtb", [32, 1], F32)
        s_alog = sb("s_alog", [32, 1], F32)
        s_negA = sb("s_negA", [32, 1], F32)
        s_dtT = sb("s_dtT", [32, TPC], F32)
        s_z32 = sb("s_z32", [32, 128], F32)
        s_gtm = sb("s_gtm", [128, NCH * 32], F32)
        s_gacc = sb("s_gacc", [128, 32], F32)
        s_o32 = sb("s_o32", [32, 128], F32)
        winb = sb("winb", [128, NDC * 32], BF16)
        wkb_ = sb("wkb", [128, 2, 8 * 512], BF16)
        wkb = lambda i: wkb_[:, i]
        kcb_ = sb("kcb", [128, 4, KVH * 128], BF16)
        kcb = lambda i: kcb_[:, i]
        vcb_ = sb("vcb", [128, 4, KVH * 128], BF16)
        vcb = lambda i: vcb_[:, i]
        xcb_ = sb("xcb", [128, 2, 128], BF16)
        xcb = lambda i: xcb_[:, i]
        kdb_ = sb("kdb", [128, 2, 128], BF16)
        kdb = lambda i: kdb_[:, i]
        rotb = sb("rotb", [128, 512], F32)
        ktmp = sb("ktmp", [128, 512], F32)
        spT_ = sb("spT", [32, 3, 512], F32)
        spT = lambda i: spT_[:, i]
        diag = sb("diag", [32, 32], F32)
        glmg = sb("glmg", [128, 32], F32)
        p_pr = [self.ps(f"p_pr{i}", st) for i in range(4)]
        p_cs = [self.ps(f"p_cs{i}", st) for i in range(2)]
        p_sm = [self.ps(f"p_sm{i}", st) for i in range(2)]

        def body(em):
            cl0 = b.constok
            for dst, src in [(s_identf[:, :], b.identf), (s_dtb[:, :], b.dtb),
                             (s_alog[:, :], b.alog), (b.s_mP[:, :], b.mcol),
                             (b.s_imP[:, :], b.invmcol), (s_o32[:, :], b.ones32c)]:
                self.load_const(em, dst, src)
            self.load_const(em, s_costm[:, :, :],
                            b.cos_tm.rearrange("(c p) e -> p c e", p=128))
            self.load_const(em, s_sintm[:, :, :],
                            b.sin_tm.rearrange("(c p) e -> p c e", p=128))
            zinit = em.do("gpsimd", lambda e: e.memset(s_z32[:, :], 0.0), inc="pool")
            gainit = em.do("gpsimd", lambda e: e.memset(s_gacc[:, :], 0.0), inc="pool")
            cl = em.tok("l")
            xw = merge_tok(*b.xw)
            xws = b.xw

            wt = em.do("gpsimd", lambda e: e.dma_start(out=winb[:, :], in_=b.win),
                       inc="w", inc_n=16)
            dt_done = []
            prev_t1 = None
            prev_t5 = None
            prev_zbtap = None
            for ts in range(2):
                bank = p_cs[ts]
                mm = None
                for d in range(NDC):
                    mm = em.do("tensor", lambda e, d=d, ts=ts, bank=bank: e.matmul(
                        bank.ap[0:32, :],
                        winb[:, d * 32:(d + 1) * 32],
                        b.xfm[:, d * TPC + ts * 512: d * TPC + (ts + 1) * 512],
                        start=(d == 0), stop=(d == NDC - 1)),
                        waits=[xws[d // 8], wt] + (bank.wr_waits() if d == 0 else []), inc="pe")
                bank.set_writer(mm)
                t0 = em.do("vector", lambda e, bank=bank: e.tensor_scalar(
                    spT(0)[:, :], bank.ap[0:32, :], s_dtb[:, :], None, ALU.add),
                    waits=[cl, prev_t1, prev_zbtap] + bank.rd_waits(), inc="dve")
                bank.add_reader(t0)
                zbtap = None
                if "zb" in b.taps:
                    zbtap = em.do("sync", lambda e, ts=ts: e.dma_start(
                        out=b.dT["zb"][:, ts * 512:(ts + 1) * 512], in_=spT(0)[:, :]),
                        waits=[t0], inc="s", inc_n=16)
                t1 = em.do("scalar", lambda e: e.activation(
                    spT(1)[:, :], spT(0)[:, :], AF.Abs),
                    waits=[t0, prev_t5], inc="act")
                prev_t1 = t1
                t2 = em.do("scalar", lambda e: e.activation(
                    spT(1)[:, :], spT(1)[:, :], AF.Exp, scale=-1.0),
                    waits=[t1], inc="act")
                t3 = em.do("scalar", lambda e: e.activation(
                    spT(1)[:, :], spT(1)[:, :], AF.Ln, bias=1.0),
                    waits=[t2], inc="act")
                t4 = em.do("vector", lambda e: e.tensor_relu(
                    spT(2)[:, :], spT(0)[:, :]),
                    waits=[t0, t2], inc="dve")
                w_extra = []
                if "sp1" in b.taps:
                    w_extra.append(em.do("sync", lambda e, ts=ts: e.dma_start(
                        out=b.dT["sp1"][:, ts * 512:(ts + 1) * 512], in_=spT(1)[:, :]),
                        waits=[t3], inc="s", inc_n=16))
                if "sp2" in b.taps:
                    w_extra.append(em.do("sync", lambda e, ts=ts: e.dma_start(
                        out=b.dT["sp2"][:, ts * 512:(ts + 1) * 512], in_=spT(2)[:, :]),
                        waits=[t4], inc="s", inc_n=16))
                t5 = em.do("vector", lambda e, ts=ts: e.tensor_add(
                    s_dtT[:, ts * 512:(ts + 1) * 512], spT(2)[:, :], spT(1)[:, :]),
                    waits=[t3, t4] + w_extra, inc="dve")
                prev_t5 = t5
                prev_zbtap = zbtap
                dt_done.append(t5)
            a1 = em.do("scalar", lambda e: e.activation(
                s_negA[:, :], s_alog[:, :], AF.Exp), waits=[cl], inc="act")
            a2 = em.do("vector", lambda e: e.tensor_scalar(
                s_negA[:, :], s_negA[:, :], -1.0, None, ALU.mult),
                waits=[a1], inc="dve")
            gT = em.do("vector", lambda e: e.tensor_scalar(
                b.s_gT[:, :], s_dtT[:, :], s_negA[:, :], None, ALU.mult),
                waits=dt_done + [a2], inc="dve")
            if "dtT" in b.taps:
                em.do("sync", lambda e: e.dma_start(out=b.dT["dtT"], in_=s_dtT[:, :]),
                      waits=[gT], inc="s", inc_n=16)

            gm = []
            self._gmst = dict(last_sc=gT, glmm_prev=None, e3_prev=None)

            def emit_gmach(c):
                last_sc = self._gmst["last_sc"]
                glmm_prev = self._gmst["glmm_prev"]
                e3_prev = self._gmst["e3_prev"]
                sc = em.do("vector", lambda e, c=c: e.tensor_tensor_scan(
                    b.s_gT[:, c * 128:(c + 1) * 128],
                    b.s_gT[:, c * 128:(c + 1) * 128], s_z32[:, :],
                    0.0, ALU.add, ALU.add), waits=[last_sc, zinit], inc="dve")
                bank = p_sm[c % 2]
                tp1 = em.do("tensor", lambda e, c=c, bank=bank: e.transpose(
                    bank.ap[:, 0:32], b.s_gT[:, c * 128:(c + 1) * 128],
                    s_identf[:, :]), waits=[sc, cl] + bank.wr_waits(), inc="pe")
                bank.set_writer(tp1)
                cp1 = em.do("scalar", lambda e, c=c, bank=bank: e.copy(
                    s_gtm[:, c * 32:(c + 1) * 32], bank.ap[:, 0:32]),
                    waits=bank.rd_waits(), inc="act")
                bank.add_reader(cp1)
                tp2 = em.do("tensor", lambda e, c=c, bank=bank: e.transpose(
                    bank.ap[:, 64:96], s_dtT[:, c * 128:(c + 1) * 128],
                    s_identf[:, :]),
                    waits=[dt_done[c // 4], cp1] + bank.wr_waits(), inc="pe")
                bank.set_writer(tp2)
                cp2 = em.do("scalar", lambda e, c=c, bank=bank: e.copy(
                    b.s_dttm[:, c * 32:(c + 1) * 32], bank.ap[:, 64:96]),
                    waits=bank.rd_waits(), inc="act")
                bank.add_reader(cp2)
                ng = em.do("scalar", lambda e, c=c: e.activation(
                    b.s_ngtm[:, c * 32:(c + 1) * 32],
                    s_gtm[:, c * 32:(c + 1) * 32], AF.Copy, scale=-1.0),
                    waits=[cp1], inc="act")
                dg = em.do("vector", lambda e, c=c: e.tensor_scalar(
                    diag[:, :], s_identf[:, :],
                    b.s_gT[:, c * 128 + 127: c * 128 + 128], None, ALU.mult),
                    waits=[sc, cl, glmm_prev], inc="dve")
                bank2 = p_sm[(c + 1) % 2]
                glmm = em.do("tensor", lambda e, bank2=bank2: e.matmul(
                    bank2.ap[:, 96:128], s_o32[:, :], diag[:, :],
                    start=True, stop=True),
                    waits=[dg, cl] + bank2.wr_waits(), inc="pe")
                bank2.set_writer(glmm)
                e1 = em.do("scalar", lambda e, c=c, bank2=bank2: e.activation(
                    b.s_egl[:, c * 32:(c + 1) * 32], bank2.ap[:, 96:128], AF.Exp),
                    waits=bank2.rd_waits(), inc="act")
                bank2.add_reader(e1)
                e2 = em.do("vector", lambda e, c=c, bank2=bank2: e.tensor_sub(
                    glmg[:, :], bank2.ap[:, 96:128],
                    s_gtm[:, c * 32:(c + 1) * 32]),
                    waits=[cp1, e3_prev] + bank2.rd_waits(), inc="dve")
                bank2.add_reader(e2)
                e3 = em.do("scalar", lambda e, c=c: e.activation(
                    b.s_eglg[:, c * 32:(c + 1) * 32], glmg[:, :], AF.Exp),
                    waits=[e2], inc="act")
                e4 = em.do("vector", lambda e, c=c, bank2=bank2: e.tensor_add(
                    s_gacc[:, :], s_gacc[:, :], bank2.ap[:, 96:128]),
                    waits=[gainit] + bank2.rd_waits(), inc="dve")
                bank2.add_reader(e4)
                gm.append(dict(dttm=cp2, ng=ng, egl=e1, eglg=e3, gacc=e4, gtm=cp1))
                self._gmst.update(last_sc=sc, glmm_prev=glmm, e3_prev=e3)

            NBLK = 32

            def wsrc(bi):
                return (bi // 16, (bi // 8) % 2, (bi // 4) % 2, bi % 4)

            def wload(bi, deps):
                half, which, es_, dq = wsrc(bi)
                src = b.wk if which == 0 else b.wv
                return em.do("gpsimd", lambda e, bi=bi, es_=es_, dq=dq, src=src: e.dma_start(
                    out=wkb(bi % 2)[:, :],
                    in_=src[es_, :, dq * 8 * 512:(dq + 1) * 8 * 512]),
                    waits=deps, inc="w", inc_n=16)

            for c_ in range(NCH):
                emit_gmach(c_)
            wtok = {0: wload(0, []), 1: wload(1, [])}
            wlast = {0: None, 1: None}
            spills = []
            kvbuf_free = [None] * 4
            prev_rope = None
            for half in range(2):
                kv_done = [[] for _ in range(4)]
                for blk in range(16):
                    bi = half * 16 + blk
                    _, which, es_, dq = wsrc(bi)
                    slot = bi % 2
                    mm_last = None
                    for d8 in range(8):
                        d = dq * 8 + d8
                        for ci in range(4):
                            c = half * 4 + ci
                            bank = p_pr[ci]
                            w0 = [xws[d // 8], wtok[bi]]
                            if d == 0:
                                w0 += bank.wr_waits()
                            mm_last = em.do("tensor", lambda e, d=d, d8=d8, c=c, slot=slot, bank=bank: e.matmul(
                                bank.ap[:, :],
                                b.xfm[:, d * TPC + c * 128: d * TPC + (c + 1) * 128],
                                wkb(slot)[:, d8 * 512:(d8 + 1) * 512],
                                start=(d == 0), stop=(d == NDC - 1)),
                                waits=w0, inc="pe")
                    wlast[slot] = mm_last
                    if bi + 2 < NBLK:
                        wtok[bi + 2] = wload(bi + 2, [wlast[slot]])
                    if dq == 3:
                        import os as _os
                        krope = _os.environ.get("KROPE", "1") == "1"
                        for ci in range(4):
                            c = half * 4 + ci
                            bank = p_pr[ci]
                            bank.set_writer(em.tok("pe"))
                            if which == 1 or not krope:
                                dstb = vcb if which == 1 else kcb
                                tv = em.do("scalar", lambda e, ci=ci, es_=es_, bank=bank, dstb=dstb: e.copy(
                                    dstb(ci)[:, es_ * 512:(es_ + 1) * 512], bank.ap[:, :]),
                                    waits=bank.rd_waits() + [kvbuf_free[ci]], inc="act")
                                bank.add_reader(tv)
                                kv_done[ci].append(tv)
                            else:
                                ktv = ktmp[:, :].rearrange("p (bb e) -> p bb e", bb=4)
                                rv = rotb[:, :].rearrange("p (bb e) -> p bb e", bb=4)
                                e0 = em.do("scalar", lambda e, bank=bank: e.copy(
                                    ktmp[:, :], bank.ap[:, :]),
                                    waits=bank.rd_waits() + [prev_rope], inc="act")
                                bank.add_reader(e0)
                                r1 = em.do("scalar", lambda e, ktv=ktv, rv=rv: e.activation(
                                    rv[:, :, 0:64], ktv[:, :, 64:128], AF.Copy, scale=-1.0),
                                    waits=[e0, prev_rope], inc="act")
                                r2 = em.do("scalar", lambda e, ktv=ktv, rv=rv: e.copy(
                                    rv[:, :, 64:128], ktv[:, :, 0:64]),
                                    waits=[e0, r1], inc="act")
                                cosr = s_costm[:, c].unsqueeze(1).broadcast_to([128, 4, 128])
                                sinr = s_sintm[:, c].unsqueeze(1).broadcast_to([128, 4, 128])
                                t1 = em.do("vector", lambda e, ktv=ktv, cosr=cosr: e.tensor_tensor(
                                    ktv, ktv, cosr, ALU.mult),
                                    waits=[e0, r1, r2, cl], inc="dve")
                                t2 = em.do("vector", lambda e, rv=rv, sinr=sinr: e.tensor_tensor(
                                    rv, rv, sinr, ALU.mult),
                                    waits=[r1, r2, cl], inc="dve")
                                t3 = em.do("vector", lambda e, ci=ci, es_=es_: e.tensor_add(
                                    kcb(ci)[:, es_ * 512:(es_ + 1) * 512],
                                    ktmp[:, :], rotb[:, :]),
                                    waits=[t1, t2, kvbuf_free[ci]], inc="dve")
                                prev_rope = t3
                                kv_done[ci].append(t3)
                for ci in range(4):
                    c = half * 4 + ci
                    kvd = kv_done[ci]
                    s1 = em.do("sync", lambda e, ci=ci, c=c: e.dma_start(
                        out=b.k_dram[c * 128:(c + 1) * 128, :], in_=kcb(ci)[:, :]),
                        waits=kvd, inc="s", inc_n=16)
                    s2 = em.do("sync", lambda e, ci=ci, c=c: e.dma_start(
                        out=b.v_dram[c * 128:(c + 1) * 128, :], in_=vcb(ci)[:, :]),
                        waits=kvd, inc="s", inc_n=16)
                    spills += [s1, s2]
                    for h in (range(H) if level >= 3 else []):
                        kv = h // 4
                        par = h % 2
                        bank = p_cs[par]
                        xc = em.do("vector", lambda e, ci=ci, kv=kv, c=c, h=h, par=par: e.tensor_scalar(
                            xcb(par)[:, :], vcb(ci)[:, kv * 128:(kv + 1) * 128],
                            b.s_dttm[:, c * 32 + h: c * 32 + h + 1], None, ALU.mult),
                            waits=kvd + [gm[c]["dttm"]] + ([bank.writer] if bank.writer else []),
                            inc="dve")
                        kd = em.do("vector", lambda e, ci=ci, kv=kv, c=c, h=h, par=par: e.tensor_scalar(
                            kdb(par)[:, :], kcb(ci)[:, kv * 128:(kv + 1) * 128],
                            b.s_eglg[:, c * 32 + h: c * 32 + h + 1], None, ALU.mult),
                            waits=kvd + [gm[c]["eglg"]], inc="dve")
                        cs = em.do("tensor", lambda e, par=par, bank=bank: e.matmul(
                            bank.ap[:, 0:128], kdb(par)[:, :], xcb(par)[:, :],
                            start=True, stop=True),
                            waits=[xc, kd] + bank.wr_waits(), inc="pe")
                        bank.set_writer(cs)
                        stt = em.do("vector", lambda e, c=c, h=h, bank=bank: e.scalar_tensor_tensor(
                            b.states[:, h * 129: h * 129 + 128],
                            b.states[:, h * 129: h * 129 + 128],
                            b.s_egl[:, c * 32 + h: c * 32 + h + 1],
                            bank.ap[:, 0:128], ALU.mult, ALU.add),
                            waits=[gm[c]["egl"], b.pinit] + bank.rd_waits(), inc="dve")
                        bank.add_reader(stt)
                    kvbuf_free[ci] = merge_tok(s1, s2, em.tok("dve"))

            for tn, key in [("GT", None), ("gtm", "gtm"), ("eglg", "eglg"), ("egl", "egl")]:
                if tn in b.taps:
                    srcs = {"GT": b.s_gT, "gtm": s_gtm, "eglg": b.s_eglg, "egl": b.s_egl}
                    w = [self._gmst["last_sc"]] if key is None else [t[key] for t in gm]
                    em.do("sync", lambda e, tn=tn, srcs=srcs: e.dma_start(
                        out=b.dT[tn], in_=srcs[tn][:, :]), waits=w, inc="s", inc_n=16)

            if level < 2:
                em.do("sync", lambda e: e.nop(),
                      waits=[em.tok("s"), em.tok("l"), em.tok("w")])
                return
            eg = em.do("scalar", lambda e: e.activation(
                s_gacc[:, :], s_gacc[:, :], AF.Exp),
                waits=[gm[NCH - 1]["gacc"]], inc="act")
            dcw = em.do("vector", lambda e: e.tensor_copy(
                b.states[:, :].rearrange("p (h q) -> p h q", h=H)[:, :, 128:129],
                s_gacc[:, :].unsqueeze(2)),
                waits=[eg, em.tok("dve")], inc="dve")
            if "payload" in b.taps:
                em.do("sync", lambda e: e.dma_start(out=b.dT["payload"], in_=b.states[:, :]),
                      waits=[dcw], inc="s", inc_n=16)
            if level < 4:
                for tn in ("kd", "vd"):
                    if tn in b.taps:
                        em.do("sync", lambda e, tn=tn: e.dma_start(
                            out=b.dT[tn], in_=(b.k_dram if tn == "kd" else b.v_dram)[:, :]),
                            waits=spills, inc="s", inc_n=16)
                em.do("sync", lambda e: e.nop(),
                      waits=[em.tok("s"), em.tok("l"), em.tok("w")])
                return
            pw = em.do("sync", lambda e: e.dma_start(out=b.cc_in[:, :], in_=b.states[:, :]),
                       waits=[dcw], inc="s", inc_n=16)
            b.pw_tok = pw
            b.mask_cl = em.tok("l")
            if "kd" in b.taps:
                em.do("sync", lambda e: e.dma_start(out=b.dT["kd"], in_=b.k_dram[:, :]),
                      waits=spills, inc="s", inc_n=16)
            if "vd" in b.taps:
                em.do("sync", lambda e: e.dma_start(out=b.dT["vd"], in_=b.v_dram[:, :]),
                      waits=spills, inc="s", inc_n=16)
            em.do("sync", lambda e: e.nop(),
                  waits=[em.tok("s"), em.tok("l"), em.tok("w")])
        self.run_block(body)
        st.close()

    def phaseB(self, yall):
        b = self
        st = contextlib.ExitStack()
        sb = lambda n, shp, dt: self.sb(n, shp, dt, st)
        s_hb_ = sb("s_hb", [32, 2, 128], F32)
        s_hb = lambda i: s_hb_[:, i]
        s_cosT = sb("s_cosT", [128, TPC], BF16)
        s_sinT = sb("s_sinT", [128, TPC], BF16)
        s_oc = sb("s_oc", [128, 1], BF16)
        s_orb = sb("s_orb", [1, 128], BF16)
        k_kv_ = sb("k_kv", [128, 2, NCH, 128], BF16)
        k_kv = lambda i: k_kv_[:, i]
        v_kv_ = sb("v_kv", [128, 1, NCH, 128], BF16)
        v_kv = lambda i: v_kv_[:, 0]
        kfm_ = sb("kfm", [128, 2, TPC], BF16)
        kfm = lambda i: kfm_[:, i]
        wqb_ = sb("wqb", [128, 2, 2048], BF16)
        wqb = lambda i: wqb_[:, i]
        qfm_ = sb("qfm", [128, 2, TPC], BF16)
        qfm = lambda i: qfm_[:, i]
        qtmp = sb("qtmp", [128, 512], F32)
        qrot = sb("qrot", [128, 512], F32)
        dcy_ = sb("dcy", [128, 2, NCH * 128], F32)
        dcy = lambda i: dcy_[:, i]
        egr_ = sb("egr", [128, 2, NCH * 128], BF16)
        egr = lambda i: egr_[:, i]
        mtb_ = sb("mtb", [128, 2, 128], BF16)
        mtb = lambda i: mtb_[:, i]
        qgb_ = sb("qgb", [128, 2, 128], BF16)
        qgb = lambda i: qgb_[:, i]
        xcb_ = sb("xcbB", [128, 2, 128], BF16)
        xcb = lambda i: xcb_[:, i]
        kdb_ = sb("kdbB", [128, 2, 128], BF16)
        kdb = lambda i: kdb_[:, i]
        stb_ = sb("stb", [128, 2, 128], BF16)
        stb = lambda i: stb_[:, i]
        gbuf = sb("gbuf", [128, (H // 8) * 129], F32)
        dcolb = sb("dcolb", [128, 1], F32)
        y2b = sb("y2b", [128, 512], BF16)
        rsrow = sb("rsrow", [1, 512], F32)
        rsbf = sb("rsbf", [1, 512], BF16)
        rbbf = sb("rbbf", [128, 512], BF16)
        p_q = [self.ps(f"pB_q{i}", st) for i in range(2)]
        p_y = [self.ps(f"pB_y{i}", st) for i in range(2)]
        p_g_ = self.ps("pB_g", st)
        p_g = [p_g_, p_g_]
        p_gn = self.ps("pB_gn", st)
        p_st = self.ps("pB_st", st)
        p_cs = self.ps("pB_cs", st)

        def body(em):
            for dst, srcd in [(s_cosT[:, :], b.cosT),
                              (s_sinT[:, :], b.sinT), (s_oc[:, :], b.onescol_bf),
                              (s_orb[:, :], b.onesrow_bf)]:
                self.load_const(em, dst, srcd)
            cl = em.tok("l")
            qctr = [0]
            wqtok = {}
            wqcons = {0: b.pw_tok, 1: b.pw_tok}

            def wq_load():
                i = qctr[0]
                qctr[0] += 1
                h, hf = divmod(i, 2)
                slot = i % 2
                wqtok[i] = em.do("gpsimd", lambda e, h=h, hf=hf, slot=slot: e.dma_start(
                    out=wqb(slot)[:, :], in_=b.wq[h, :, hf * 2048:(hf + 1) * 2048]),
                    waits=[wqcons[slot]], inc="w", inc_n=16)
            wq_load()
            wq_load()
            z2 = em.do("gpsimd", lambda e: e.memset(b.states[:, :], 0.0),
                       waits=[b.pw_tok], inc="pool")
            cct = em.do("gpsimd", lambda e: e.collective_compute(
                "AllGather", ALU.bypass,
                replica_groups=[[0, 1, 2, 3], [4, 5, 6, 7]],
                ins=[b.cc_in.ap().opt()], outs=[b.cc_out.ap().opt()]),
                waits=[b.pw_tok], inc="cc")
            def emit_combine():
                prev_dve = None
                HH = H // 8
                for r in range(4):
                    for sub in range(8):
                        gld = em.do("sync", lambda e, r=r, sub=sub: e.dma_start(
                            out=gbuf[:, :],
                            in_=b.cc_out[r * 128:(r + 1) * 128,
                                         sub * HH * 129:(sub + 1) * HH * 129]),
                            waits=[cct, prev_dve], inc="l", inc_n=16)
                        mk = em.do("vector", lambda e, r=r: e.tensor_scalar(
                            gbuf[:, :], gbuf[:, :],
                            b.s_mP[:, r:r + 1], None, ALU.mult),
                            waits=[gld, b.mask_cl], inc="dve")
                        for hh in range(HH):
                            h = sub * HH + hh
                            dc = em.do("vector", lambda e, hh=hh, r=r: e.tensor_scalar(
                                dcolb[:, :], gbuf[:, hh * 129 + 128: hh * 129 + 129],
                                b.s_imP[:, r:r + 1], None, ALU.add), waits=[mk], inc="dve")
                            em.do("vector", lambda e, hh=hh, h=h: e.scalar_tensor_tensor(
                                b.states[:, h * 129: h * 129 + 128],
                                b.states[:, h * 129: h * 129 + 128],
                                dcolb[:, :],
                                gbuf[:, hh * 129: hh * 129 + 128],
                                ALU.mult, ALU.add), waits=[dc, z2], inc="dve")
                        prev_dve = em.tok("dve")
                b.sin_done = em.tok("dve")
                if "sinB" in b.taps:
                    em.do("sync", lambda e: e.dma_start(out=b.dT["sinB"], in_=b.states[:, :]),
                          waits=[b.sin_done], inc="s", inc_n=16)

            yg_toks = []

            def emit_qslice(h, sl8):
                for d in range(sl8 * 4, sl8 * 4 + 4):
                    i = h * 2 + d // 16
                    slot = i % 2
                    for ts in range(2):
                        bank = p_q[ts]
                        w0 = [wqtok[i]]
                        if d == 0:
                            w0 += bank.wr_waits()
                        self._qmm[ts] = em.do("tensor", lambda e, d=d, ts=ts, slot=slot, bank=bank: e.matmul(
                            bank.ap[:, :],
                            wqb(slot)[:, (d % 16) * 128:(d % 16 + 1) * 128],
                            b.xfm[:, d * TPC + ts * 512: d * TPC + (ts + 1) * 512],
                            start=(d == 0), stop=(d == NDC - 1)),
                            waits=w0, inc="pe")
                if sl8 in (3, 7):
                    i = h * 2 + sl8 // 4
                    wqcons[i % 2] = self._qmm[1]
                    if qctr[0] < H * 2:
                        wq_load()

            def emit_rope(h):
                hp = h % 2
                rope_done = []
                for ts in range(2):
                    bank = p_q[ts]
                    bank.set_writer(self._qmm[ts])
                    e0 = em.do("scalar", lambda e, bank=bank: e.copy(
                        qtmp[:, :], bank.ap[:, :]),
                        waits=bank.rd_waits() + [self._qrope], inc="act")
                    bank.add_reader(e0)
                    r1 = em.do("scalar", lambda e: e.activation(
                        qrot[0:64, :], qtmp[64:128, :], AF.Copy, scale=-1.0),
                        waits=[e0, self._qrope], inc="act")
                    r2 = em.do("scalar", lambda e: e.copy(
                        qrot[64:128, :], qtmp[0:64, :]), waits=[e0, r1], inc="act")
                    t1 = em.do("vector", lambda e, ts=ts: e.tensor_tensor(
                        qtmp[:, :], qtmp[:, :],
                        s_cosT[:, ts * 512:(ts + 1) * 512], ALU.mult),
                        waits=[e0, r1, r2, cl], inc="dve")
                    t2 = em.do("vector", lambda e, ts=ts: e.tensor_tensor(
                        qrot[:, :], qrot[:, :],
                        s_sinT[:, ts * 512:(ts + 1) * 512], ALU.mult),
                        waits=[r1, r2, cl], inc="dve")
                    t3 = em.do("vector", lambda e, ts=ts, hp=hp: e.tensor_add(
                        qfm(hp)[:, ts * 512:(ts + 1) * 512], qtmp[:, :], qrot[:, :]),
                        waits=[t1, t2, self._qfree[hp]], inc="dve")
                    self._qrope = t3
                    rope_done.append(t3)
                return rope_done

            def emit_decay(h, c):
                hp = h % 2
                gb = p_g[0]
                hbs = s_hb(hp)
                gw = em.do("tensor", lambda e, c=c, h=h, hbs=hbs, gb=gb: e.matmul(
                    gb.ap[:, 0:128], hbs[:, :],
                    b.s_gT[:, c * 128:(c + 1) * 128],
                    start=True, stop=True),
                    waits=[cl, self._hbt[hp]] + gb.wr_waits(), inc="pe")
                gb.set_writer(gw)
                dm = em.do("vector", lambda e, c=c, hp=hp, gb=gb: e.tensor_add(
                    dcy(hp)[:, c * 128:(c + 1) * 128], gb.ap[:, 0:128], b.s_negm[:, :]),
                    waits=gb.rd_waits() + [self._dcyfree[hp]], inc="dve")
                gb.add_reader(dm)
                dk = em.do("scalar", lambda e, c=c, h=h, hp=hp: e.activation(
                    dcy(hp)[:, c * 128:(c + 1) * 128],
                    dcy(hp)[:, c * 128:(c + 1) * 128], AF.Exp,
                    bias=b.s_ngtm[:, c * 32 + h: c * 32 + h + 1]),
                    waits=[dm], inc="act")
                eg2 = em.do("scalar", lambda e, c=c, hp=hp, gb=gb: e.activation(
                    egr(hp)[:, c * 128:(c + 1) * 128], gb.ap[:, 0:128], AF.Exp),
                    waits=gb.rd_waits() + [self._egrfree[hp]], inc="act")
                gb.add_reader(eg2)
                self._dk[(h, c)] = dk
                self._eg2[(h, c)] = eg2

            def load_hb(h):
                hp = h % 2
                self._hbt[hp] = em.do("sync", lambda e, h=h, hp=hp: e.dma_start(
                    out=s_hb(hp)[:, :], in_=b.hbc[:, h * 128:(h + 1) * 128]),
                    waits=[self._hbfree[hp]], inc="l", inc_n=16)

            def gn_pump():
                if not self._gnq:
                    return
                j = self._gnq[0]
                s = j["st"]
                j["st"] += 1
                yb = j["yb"]
                if s == 0:
                    j["sq"] = em.do("scalar", lambda e, yb=yb: e.activation(
                        y2b[:, :], yb.ap[:, :], AF.Square),
                        waits=yb.rd_waits() + [self._y2free], inc="act")
                    yb.add_reader(j["sq"])
                elif s == 1:
                    j["smq"] = em.do("tensor", lambda e: e.matmul(
                        p_gn.ap[0:1, :], s_oc[:, :], y2b[:, :],
                        start=True, stop=True),
                        waits=[j["sq"], cl] + p_gn.wr_waits(), inc="pe")
                    p_gn.set_writer(j["smq"])
                    self._y2free = j["smq"]
                    rs1 = em.do("scalar", lambda e: e.activation(
                        rsrow[:, :], p_gn.ap[0:1, :], AF.Ln,
                        scale=1.0 / 128.0, bias=b._s_eps2[0:1, :]),
                        waits=p_gn.rd_waits() + [self._rsfree], inc="act")
                    p_gn.add_reader(rs1)
                    j["rs2"] = em.do("scalar", lambda e: e.activation(
                        rsbf[:, :], rsrow[:, :], AF.Exp, scale=-0.5),
                        waits=[rs1, self._rsbffree], inc="act")
                    self._rsfree = j["rs2"]
                else:
                    bcm = em.do("tensor", lambda e: e.matmul(
                        p_gn.ap[:, :], s_orb[:, :], rsbf[:, :],
                        start=True, stop=True),
                        waits=[j["rs2"], cl] + p_gn.wr_waits(), inc="pe")
                    p_gn.set_writer(bcm)
                    self._rsbffree = bcm
                    rb = em.do("scalar", lambda e: e.activation(
                        rbbf[:, :], p_gn.ap[:, :], AF.Copy),
                        waits=p_gn.rd_waits() + [self._rbfree], inc="act")
                    p_gn.add_reader(rb)
                    yfin = em.do("vector", lambda e, h=j["h"], cg=j["cg"], yb=yb: e.scalar_tensor_tensor(
                        yall[:, h * TPC + cg * 512: h * TPC + (cg + 1) * 512],
                        yb.ap[:, :], b.s_gnw[:, :], rbbf[:, :],
                        ALU.mult, ALU.mult),
                        waits=[rb, cl] + yb.rd_waits(), inc="dve")
                    yb.add_reader(yfin)
                    self._rbfree = yfin
                    yg_toks.append(yfin)
                    self._gnq.popleft()

            load_hb(0)
            em.do("sync", lambda e: e.nop())
            for sl8 in range(8):
                emit_qslice(0, sl8)
            self._ropes = {0: emit_rope(0)}
            for c in range(NCH):
                emit_decay(0, c)
            self._hbfree[0] = em.tok("pe")
            if "dcy0" in b.taps:
                em.do("sync", lambda e: e.dma_start(out=b.dT["dcy0"], in_=dcy(0)[:, :]),
                      waits=[self._dk[(0, c)] for c in range(NCH)], inc="s", inc_n=16)
                em.do("sync", lambda e: e.dma_start(out=b.dT["egr0"], in_=egr(0)[:, :]),
                      waits=[self._eg2[(0, c)] for c in range(NCH)], inc="s", inc_n=16)
            load_hb(1)

            self._lkt, self._ltt, self._lvt = {}, {}, {}

            def kv_load_k(kv):
                kp = kv % 2
                self._lkt[kv] = em.do("sync", lambda e, kv=kv, kp=kp: e.dma_start(
                    out=k_kv(kp)[:, :, :],
                    in_=b.k_dram[:, kv * 128:(kv + 1) * 128]
                        .rearrange("(c p) e -> p c e", p=128)),
                    waits=[self._kvfree[kp]], inc="l", inc_n=16)
                self._ltt[kv] = em.do("sync", lambda e, kv=kv, kp=kp: e.dma_start(
                    out=kfm(kp)[:, :], in_=b.k_dram[:, kv * 128:(kv + 1) * 128],
                    transpose=True),
                    waits=[self._kvfree[kp]], inc="l", inc_n=16)

            def kv_load_v(kv):
                self._lvt[kv] = em.do("sync", lambda e, kv=kv: e.dma_start(
                    out=v_kv(0)[:, :, :],
                    in_=b.v_dram[:, kv * 128:(kv + 1) * 128]
                        .rearrange("(c p) e -> p c e", p=128)),
                    waits=[self._vkvfree], inc="l", inc_n=16)

            kv_load_k(0)
            kv_load_k(1)
            kv_load_v(0)
            emit_combine()
            sin_tok = b.sin_done
            for kv in range(KVH):
                kp = kv % 2
                lk = self._lkt[kv]
                lv = self._lvt[kv]
                lt = self._ltt[kv]
                for hh in range(4):
                    h = kv * 4 + hh
                    hp = h % 2
                    rope_done = self._ropes.pop(h)
                    if h == 0 and "qfm0" in b.taps:
                        em.do("sync", lambda e: e.dma_start(
                            out=b.dT["qfm0"], in_=qfm(0)[:, :]),
                            waits=rope_done, inc="s", inc_n=16)
                    stt_prev = sin_tok
                    self._xckd[0] = (
                        em.do("vector", lambda e, h=h, kp=kp: e.tensor_scalar(
                            xcb(0)[:, :], v_kv(kp)[:, 0, :],
                            b.s_dttm[:, h: h + 1], None, ALU.mult),
                            waits=[lv, self._xcfree[0]], inc="dve"),
                        em.do("vector", lambda e, h=h, kp=kp: e.tensor_scalar(
                            kdb(0)[:, :], k_kv(kp)[:, 0, :],
                            b.s_eglg[:, h: h + 1], None, ALU.mult),
                            waits=[lk, self._kdfree[0]], inc="dve"))
                    ylast = []
                    import os as _os3
                    _ser = _os3.environ.get("KSER", "") == "h0" and h == 0
                    for c in range(NCH):
                        if _ser:
                            allw = [em.tok(s) for s in ("pe", "dve", "act", "l", "w", "s", "pool")]
                            em.do("tensor", lambda e: e.nop(), waits=allw)
                            em.do("vector", lambda e: e.nop(), waits=allw)
                            em.do("scalar", lambda e: e.nop(), waits=allw)
                        it = c % 2
                        cg = c // 4
                        yb = p_y[cg]
                        smm = em.do("tensor", lambda e, c=c, kp=kp, hp=hp: e.matmul(
                            p_st.ap[:, 0:128], kfm(kp)[:, c * 128:(c + 1) * 128],
                            qfm(hp)[:, c * 128:(c + 1) * 128],
                            start=True, stop=True),
                            waits=[lt, rope_done[c // 4]] + p_st.wr_waits(), inc="pe")
                        p_st.set_writer(smm)
                        mt = em.do("vector", lambda e, it=it, c=c, hp=hp: e.tensor_mul(
                            mtb(it)[:, :], p_st.ap[:, 0:128],
                            dcy(hp)[:, c * 128:(c + 1) * 128]),
                            waits=[self._dk[(h, c)], self._mtfree[it]] + p_st.rd_waits(), inc="dve")
                        p_st.add_reader(mt)
                        if h == 0 and c == 2 and "mtb2" in b.taps:
                            em.do("sync", lambda e, it=it: e.dma_start(
                                out=b.dT["mtb2"], in_=mtb(it)[:, :]),
                                waits=[mt], inc="s", inc_n=16)
                        qg = em.do("vector", lambda e, it=it, c=c, hp=hp: e.tensor_mul(
                            qgb(it)[:, :], qfm(hp)[:, c * 128:(c + 1) * 128],
                            egr(hp)[:, c * 128:(c + 1) * 128]),
                            waits=[self._eg2[(h, c)], rope_done[c // 4], self._qgfree[it]], inc="dve")
                        sbc = em.do("scalar", lambda e, it=it, h=h: e.copy(
                            stb(it)[:, :], b.states[:, h * 129: h * 129 + 128]),
                            waits=[stt_prev, self._stbfree[it]], inc="act")
                        if c + 1 < NCH:
                            it2 = (c + 1) % 2
                            self._xckd[it2] = (
                                em.do("vector", lambda e, it2=it2, c=c, h=h, kp=kp: e.tensor_scalar(
                                    xcb(it2)[:, :], v_kv(kp)[:, c + 1, :],
                                    b.s_dttm[:, (c + 1) * 32 + h: (c + 1) * 32 + h + 1], None, ALU.mult),
                                    waits=[lv, self._xcfree[it2]], inc="dve"),
                                em.do("vector", lambda e, it2=it2, c=c, h=h, kp=kp: e.tensor_scalar(
                                    kdb(it2)[:, :], k_kv(kp)[:, c + 1, :],
                                    b.s_eglg[:, (c + 1) * 32 + h: (c + 1) * 32 + h + 1], None, ALU.mult),
                                    waits=[lk, self._kdfree[it2]], inc="dve"))
                        xc, kd = self._xckd[it]
                        if h + 1 < H:
                            emit_qslice(h + 1, c)
                            emit_decay(h + 1, c)
                        ym1 = em.do("tensor", lambda e, c=c, it=it, yb=yb: e.matmul(
                            yb.ap[:, (c % 4) * 128:(c % 4 + 1) * 128],
                            xcb(it)[:, :], mtb(it)[:, :],
                            start=True, stop=False),
                            waits=[xc, mt] + (yb.wr_waits() if c % 4 == 0 else []), inc="pe")
                        ym2 = em.do("tensor", lambda e, c=c, it=it, yb=yb: e.matmul(
                            yb.ap[:, (c % 4) * 128:(c % 4 + 1) * 128],
                            stb(it)[:, :], qgb(it)[:, :],
                            start=False, stop=True),
                            waits=[sbc, qg], inc="pe")
                        if c % 4 == 0:
                            yb.set_writer(ym2)
                        else:
                            yb.writer = merge_tok(yb.writer, ym2)
                        cmm = em.do("tensor", lambda e, it=it: e.matmul(
                            p_cs.ap[:, 0:128], kdb(it)[:, :], xcb(it)[:, :],
                            start=True, stop=True),
                            waits=[kd, xc] + p_cs.wr_waits(), inc="pe")
                        p_cs.set_writer(cmm)
                        stt = em.do("vector", lambda e, c=c, h=h: e.scalar_tensor_tensor(
                            b.states[:, h * 129: h * 129 + 128],
                            b.states[:, h * 129: h * 129 + 128],
                            b.s_egl[:, c * 32 + h: c * 32 + h + 1],
                            p_cs.ap[:, 0:128], ALU.mult, ALU.add),
                            waits=[cmm, sbc] + p_cs.rd_waits(), inc="dve")
                        p_cs.add_reader(stt)
                        if h == 0 and "st0" in b.taps:
                            em.do("sync", lambda e, c=c: e.dma_start(
                                out=b.dT["st0"][:, c * 128:(c + 1) * 128],
                                in_=b.states[:, 0:128]),
                                waits=[stt], inc="s", inc_n=16)
                        stt_prev = stt
                        self._mtfree[it] = ym1
                        self._qgfree[it] = ym2
                        self._xcfree[it] = merge_tok(ym1, cmm)
                        self._kdfree[it] = cmm
                        self._stbfree[it] = ym2
                        if c % 4 == 3:
                            self._gnq.append(dict(st=0, h=h, cg=cg, yb=yb))
                        gn_pump()
                    if h + 1 < H:
                        self._ropes[h + 1] = emit_rope(h + 1)
                    if h + 2 < H:
                        load_hb(h + 2)
                    self._dcyfree[hp] = em.tok("dve")
                    self._egrfree[hp] = em.tok("dve")
                    self._qfree[hp] = em.tok("pe")
                    self._hbfree[(hp + 1) % 2] = em.tok("pe")
                self._kvfree[kp] = merge_tok(em.tok("dve"), em.tok("pe"))
                self._vkvfree = merge_tok(em.tok("dve"), em.tok("pe"))
                if kv + 2 < KVH:
                    kv_load_k(kv + 2)
                if kv + 1 < KVH:
                    kv_load_v(kv + 1)
            while self._gnq:
                gn_pump()
            b.yB_done = em.tok("dve")
            if "yall" in b.taps:
                em.do("sync", lambda e: e.dma_start(out=b.dT["yall"], in_=yall[:, :]),
                      waits=yg_toks[-4:] + [em.tok("dve")], inc="s", inc_n=16)
            em.do("sync", lambda e: e.nop(),
                  waits=[em.tok("s"), em.tok("l"), em.tok("w")])
        self._qrope = None
        self._qfree = [None, None]
        self._hbfree = [None, None]
        self._kvfree = [None, None]
        self._dmfree = [None, None]
        self._dcyfree = [None, None]
        self._egrfree = [None, None]
        self._mtfree = [None, None]
        self._qgfree = [None, None]
        self._xcfree = [None, None]
        self._kdfree = [None, None]
        self._stbfree = [None, None]
        self._y2free = None
        self._rsfree = None
        self._rsbffree = None
        self._rbfree = None

        def reset_and_run(em):
            import collections as _coll
            self._gnq = _coll.deque()
            self._xckd = {0: None, 1: None}
            self._qmm = {0: None, 1: None}
            self._ropes = {}
            self._dk = {}
            self._eg2 = {}
            self._hbt = [None, None]
            self._vkvfree = None
            self._qrope = None
            self._qfree = [None, None]
            self._hbfree = [None, None]
            self._kvfree = [None, None]
            self._dmfree = [None, None]
            self._dcyfree = [None, None]
            self._egrfree = [None, None]
            self._mtfree = [None, None]
            self._qgfree = [None, None]
            self._xcfree = [None, None]
            self._kdfree = [None, None]
            self._stbfree = [None, None]
            self._y2free = None
            self._rsfree = None
            self._rsbffree = None
            self._rbfree = None
            body(em)
        self.run_block(reset_and_run)
        st.close()

    def phaseC(self, yall):
        b = self
        st = contextlib.ExitStack()
        sb = lambda n, shp, dt: self.sb(n, shp, dt, st)
        wgb_ = sb("wgb", [128, 2, DM], BF16)
        wgb = lambda i: wgb_[:, i]
        silu_bf = sb("silu_bf", [128, 512], BF16)
        p_q = [self.ps(f"pC_q{i}", st) for i in range(2)]

        def body(em):
            yprev = b.yB_done
            qctr = [0]
            wqtok = {}
            wqcons = {0: None, 1: None}

            def wg_load():
                h = qctr[0]
                qctr[0] += 1
                slot = h % 2
                wqtok[h] = em.do("gpsimd", lambda e, h=h, slot=slot: e.dma_start(
                    out=wgb(slot)[:, :], in_=b.wg[h]),
                    waits=[wqcons[slot]], inc="w", inc_n=16)
            wg_load()
            wg_load()
            prev_silu = None
            for h in range(H):
                qmm = {0: None, 1: None}
                slot = h % 2
                for d in range(NDC):
                    for ts in range(2):
                        bank = p_q[ts]
                        w0 = [wqtok[h]]
                        if d == 0:
                            w0 += bank.wr_waits()
                        qmm[ts] = em.do("tensor", lambda e, d=d, ts=ts, slot=slot, bank=bank: e.matmul(
                            bank.ap[:, :],
                            wgb(slot)[:, d * 128:(d + 1) * 128],
                            b.xfm[:, d * TPC + ts * 512: d * TPC + (ts + 1) * 512],
                            start=(d == 0), stop=(d == NDC - 1)),
                            waits=w0, inc="pe")
                wqcons[slot] = qmm[1]
                if qctr[0] < H:
                    wg_load()
                for ts in range(2):
                    bank = p_q[ts]
                    bank.set_writer(qmm[ts])
                    sl = em.do("scalar", lambda e, bank=bank: e.activation(
                        silu_bf[:, :], bank.ap[:, :], AF.Silu),
                        waits=bank.rd_waits() + [prev_silu], inc="act")
                    bank.add_reader(sl)
                    g0t = None
                    if h == 0 and "gate0" in b.taps:
                        g0t = em.do("sync", lambda e, ts=ts: e.dma_start(
                            out=b.dT["gate0"][:, ts * 512:(ts + 1) * 512], in_=silu_bf[:, :]),
                            waits=[sl], inc="s", inc_n=16)
                    gm = em.do("vector", lambda e, h=h, ts=ts: e.tensor_mul(
                        yall[:, h * TPC + ts * 512: h * TPC + (ts + 1) * 512],
                        yall[:, h * TPC + ts * 512: h * TPC + (ts + 1) * 512],
                        silu_bf[:, :]),
                        waits=[sl, yprev], inc="dve")
                    prev_silu = merge_tok(gm, g0t)
            b.yC_done = em.tok("dve")
            if "yallC" in b.taps:
                em.do("sync", lambda e: e.dma_start(out=b.dT["yallC"], in_=yall[:, :]),
                      waits=[b.yC_done], inc="s", inc_n=16)
            em.do("sync", lambda e: e.nop(), waits=[em.tok("s"), em.tok("w")])
        self.run_block(body)
        st.close()

    def phaseD(self, yall):
        b = self
        st = contextlib.ExitStack()
        sb = lambda n, shp, dt: self.sb(n, shp, dt, st)
        wob_ = sb("wob", [128, 2, 16 * 512], BF16)
        wob = lambda i: wob_[:, i]
        stage_ = sb("stageD", [128, 4, 512], F32)
        stage = lambda i: stage_[:, i]
        p_o = [self.ps(f"pD_{i}", st) for i in range(8)]

        def body(em):
            yc = b.yC_done
            wtok = {}
            wcons = {0: None, 1: None}
            nblk = 16

            def wload(i):
                o, hh = divmod(i, 2)
                slot = i % 2
                wtok[i] = em.do("gpsimd", lambda e, o=o, hh=hh, slot=slot: e.dma_start(
                    out=wob(slot)[:, :],
                    in_=b.wo[o, :, hh * 16 * 512:(hh + 1) * 16 * 512]),
                    waits=[wcons[slot]], inc="w", inc_n=16)
            wload(0)
            wload(1)
            stfree = [None, None, None, None]
            for o in range(8):
                mmlast = {}
                for hh in range(2):
                    i = o * 2 + hh
                    slot = i % 2
                    for h16 in range(16):
                        h = hh * 16 + h16
                        for c in range(8):
                            bank = p_o[c]
                            w0 = [wtok[i], yc]
                            if h == 0:
                                w0 += bank.wr_waits()
                            mmlast[c] = em.do("tensor", lambda e, h=h, h16=h16, c=c, slot=slot, bank=bank: e.matmul(
                                bank.ap[:, :],
                                yall[:, h * TPC + c * 128: h * TPC + (c + 1) * 128],
                                wob(slot)[:, h16 * 512:(h16 + 1) * 512],
                                start=(h == 0), stop=(h == H - 1)),
                                waits=w0, inc="pe")
                    wcons[slot] = mmlast[7]
                    if i + 2 < nblk:
                        wload(i + 2)
                for c in range(8):
                    bank = p_o[c]
                    bank.set_writer(mmlast[c])
                    sp = c % 4
                    ev = em.do("scalar", lambda e, c=c, sp=sp, bank=bank: e.copy(
                        stage(sp)[:, :], bank.ap[:, :]),
                        waits=bank.rd_waits() + [stfree[sp]], inc="act")
                    bank.add_reader(ev)
                    qeng, qsem = ("sync", "s") if c % 2 == 0 else ("scalar", "s2")
                    sdma = em.do(qeng, lambda e, c=c, o=o, sp=sp: e.dma_start(
                        out=b.attn_dram[c * 128:(c + 1) * 128, o * 512:(o + 1) * 512],
                        in_=stage(sp)[:, :]),
                        waits=[ev], inc=qsem, inc_n=16)
                    stfree[sp] = sdma
            b.attn_done = merge_tok(em.tok("s"), em.tok("s2"))
            if "attn" in b.taps:
                em.do("sync", lambda e: e.dma_start(out=b.dT["attn"], in_=b.attn_dram[:, :]),
                      waits=[b.attn_done], inc="s", inc_n=16)
            em.do("sync", lambda e: e.nop(),
                  waits=[em.tok("s"), em.tok("s2"), em.tok("w")])
        self.run_block(body)
        st.close()

    def _emit_ln_rows(self, em, st_bufs, t, in_a, in_b, wait_in, out_dram,
                      bf_dram=None, prev=None):
        b = self
        arow_, xrow_, cols_, x1bf_ = st_bufs
        sp = t % 2
        arow = arow_[:, sp]
        xrow = xrow_[:, sp]
        cols = cols_[:, sp]
        x1bf = x1bf_[:, sp] if x1bf_ is not None else None
        la = em.do("sync", lambda e: e.dma_start(
            out=arow[:, :], in_=in_a[t * 128:(t + 1) * 128, :]),
            waits=[wait_in] + ([prev["arow_free"]] if prev else []), inc="l", inc_n=16)
        lb = em.do("gpsimd", lambda e: e.dma_start(
            out=xrow[:, :], in_=in_b[t * 128:(t + 1) * 128, :]),
            waits=[wait_in] + ([prev["xrow_free"]] if prev else []), inc="w", inc_n=16)
        ad = em.do("vector", lambda e: e.tensor_add(arow[:, :], arow[:, :], xrow[:, :]),
                   waits=[la, lb], inc="dve")
        sq = em.do("scalar", lambda e: e.activation(
            xrow[:, :], arow[:, :], AF.Square, accum_out=cols[:, 0:1]),
            waits=[ad] + ([prev["sq_free"]] if prev else []), inc="act")
        sm = em.do("vector", lambda e: e.reduce_sum(
            cols[:, 1:2], arow[:, :], axis=mybir.AxisListType.X),
            waits=[ad] + ([prev["sm_free"]] if prev else []), inc="dve")
        m1 = em.do("vector", lambda e: e.tensor_scalar(
            cols[:, 2:3], cols[:, 1:2], 1.0 / DM, None, ALU.mult),
            waits=[sm], inc="dve")
        vv = em.do("vector", lambda e: e.tensor_mul(
            cols[:, 3:4], cols[:, 2:3], cols[:, 2:3]), waits=[m1], inc="dve")
        v1 = em.do("vector", lambda e: e.tensor_scalar(
            cols[:, 4:5], cols[:, 0:1], 1.0 / DM, None, ALU.mult),
            waits=[sq], inc="dve")
        v2 = em.do("vector", lambda e: e.tensor_sub(
            cols[:, 5:6], cols[:, 4:5], cols[:, 3:4]),
            waits=[v1, vv], inc="dve")
        r1 = em.do("scalar", lambda e: e.activation(
            cols[:, 6:7], cols[:, 5:6], AF.Ln, bias=b.s_epsB[:, :]),
            waits=[v2], inc="act")
        r2 = em.do("scalar", lambda e: e.activation(
            cols[:, 7:8], cols[:, 6:7], AF.Exp, scale=-0.5),
            waits=[r1], inc="act")
        xn = em.do("vector", lambda e: e.tensor_scalar(
            xrow[:, :], arow[:, :], cols[:, 2:3], cols[:, 7:8],
            ALU.subtract, ALU.mult), waits=[r2, sq], inc="dve")
        so = em.do("scalar", lambda e: e.dma_start(
            out=out_dram[t * 128:(t + 1) * 128, :], in_=xrow[:, :]),
            waits=[xn], inc="s2", inc_n=16)
        toks = dict(arow_free=xn, sq_free=xn, sm_free=m1, out=so)
        if bf_dram is not None:
            cb = em.do("vector", lambda e: e.tensor_copy(x1bf[:, :], xrow[:, :]),
                       waits=[xn] + ([prev["bf_free"]] if prev else []), inc="dve")
            sb2 = em.do("scalar", lambda e: e.dma_start(
                out=bf_dram[t * 128:(t + 1) * 128, :], in_=x1bf[:, :]),
                waits=[cb], inc="s2", inc_n=16)
            toks["bf_free"] = sb2
            toks["bfout"] = sb2
            toks["xrow_free"] = merge_tok(cb, so)
        else:
            toks["xrow_free"] = so
        return toks

    def phaseE(self):
        b = self
        st = contextlib.ExitStack()
        sb = lambda n, shp, dt: self.sb(n, shp, dt, st)
        arow = sb("arowE", [128, 2, DM], F32)
        xrow = sb("xrowE", [128, 2, DM], F32)
        cols = sb("colsE", [128, 2, 8], F32)
        x1bf = sb("x1bfE", [128, 2, DM], BF16)
        b.s_epsB = getattr(b, "s_epsB", None)

        def body(em):
            if b.s_epsB is None:
                b.s_epsB = b._s_eps2
            prevs = {0: None, 1: None}
            bfouts = []
            for t in range(NCH):
                prevs[t % 2] = self._emit_ln_rows(em, (arow, xrow, cols, x1bf), t,
                                          b.attn_dram, b.x_tm, b.attn_done,
                                          b.x1_dram, b.x1b_dram, prevs[t % 2])
                bfouts.append(prevs[t % 2]["bfout"])
            tts = []
            for d in range(NDC):
                tts.append(em.do("sync", lambda e, d=d: e.dma_start(
                    out=b.xfm[:, d * TPC:(d + 1) * TPC],
                    in_=b.x1b_dram[:, d * 128:(d + 1) * 128], transpose=True),
                    waits=bfouts, inc="l", inc_n=16))
            b.x1fm_done = merge_tok(*tts)
            if "xfmE" in b.taps:
                em.do("sync", lambda e: e.dma_start(out=b.dT["xfmE"], in_=b.xfm[:, :]),
                      waits=[b.x1fm_done], inc="s", inc_n=16)
            b.x1_done = merge_tok(em.tok("s"), em.tok("s2"), em.tok("w"))
            if "x1" in b.taps:
                em.do("sync", lambda e: e.dma_start(out=b.dT["x1"], in_=b.x1_dram[:, :]),
                      waits=[b.x1_done], inc="s", inc_n=16)
            em.do("sync", lambda e: e.nop(),
                  waits=[em.tok("s"), em.tok("s2"), em.tok("l"), em.tok("w")])
        self.run_block(body)
        st.close()

    def phaseF(self):
        b = self
        st = contextlib.ExitStack()
        sb = lambda n, shp, dt: self.sb(n, shp, dt, st)
        wab = sb("wab", [128, NDC * 128], BF16)
        h_bf = sb("h_bf", [128, TPC], BF16)
        bcat = sb("bcat", [128, 8 * 512], BF16)
        fbrow = sb("fbrow", [1, DM], BF16)
        s_orB = sb("s_orBF", [1, 512], BF16)
        wfb_ = sb("wfb", [128, 2, 16 * 512], BF16)
        wfb = lambda i: wfb_[:, i]
        stage_ = sb("stageF", [128, 4, 512], F32)
        stage = lambda i: stage_[:, i]
        p_o = [self.ps(f"pF_{i}", st) for i in range(8)]

        def body(em):
            x1t = b.x1fm_done
            wa_t = em.do("gpsimd", lambda e: e.dma_start(out=wab[:, :], in_=b.wa),
                         inc="w", inc_n=16)
            bc_t = em.do("gpsimd", lambda e: e.dma_start(
                out=bcat[:, :].rearrange("p (o e) -> p o e", o=8),
                in_=b.wb.rearrange("o p e -> p o e")),
                inc="w", inc_n=16)
            fb_t = em.do("gpsimd", lambda e: e.dma_start(out=fbrow[:, :], in_=b.fbias),
                         inc="w", inc_n=16)
            or_t = em.do("gpsimd", lambda e: e.dma_start(out=s_orB[:, :], in_=b.onesrow),
                         inc="w", inc_n=16)
            hdone = []
            for ts in range(2):
                bank = p_o[ts]
                mm = None
                for d in range(NDC):
                    mm = em.do("tensor", lambda e, d=d, ts=ts, bank=bank: e.matmul(
                        bank.ap[:, :], wab[:, d * 128:(d + 1) * 128],
                        b.xfm[:, d * TPC + ts * 512: d * TPC + (ts + 1) * 512],
                        start=(d == 0), stop=(d == NDC - 1)),
                        waits=[wa_t, x1t] + (bank.wr_waits() if d == 0 else []),
                        inc="pe")
                bank.set_writer(mm)
                hv = em.do("vector", lambda e, ts=ts, bank=bank: e.tensor_scalar(
                    h_bf[:, ts * 512:(ts + 1) * 512], bank.ap[:, :],
                    b.s_lwB[:, :], 2.0, ALU.mult, ALU.mult),
                    waits=bank.rd_waits(), inc="dve")
                bank.add_reader(hv)
                hdone.append(hv)
            wtok = {}
            wcons = {0: None, 1: None}

            def wload(i):
                o, dh = divmod(i, 2)
                slot = i % 2
                wtok[i] = em.do("gpsimd", lambda e, o=o, dh=dh, slot=slot: e.dma_start(
                    out=wfb(slot)[:, :],
                    in_=b.wf[o, :, dh * 16 * 512:(dh + 1) * 16 * 512]),
                    waits=[wcons[slot]], inc="w", inc_n=16)
            wload(0)
            wload(1)
            stfree = [None, None, None, None]
            for o in range(8):
                mmlast = {}
                for dh in range(2):
                    i = o * 2 + dh
                    slot = i % 2
                    for d16 in range(16):
                        d = dh * 16 + d16
                        for c in range(8):
                            bank = p_o[c]
                            w0 = [wtok[i], x1t]
                            if d == 0:
                                w0 += bank.wr_waits()
                            mmlast[c] = em.do("tensor", lambda e, d=d, d16=d16, c=c, slot=slot, bank=bank: e.matmul(
                                bank.ap[:, :],
                                b.xfm[:, d * TPC + c * 128: d * TPC + (c + 1) * 128],
                                wfb(slot)[:, d16 * 512:(d16 + 1) * 512],
                                start=(d == 0), stop=False),
                                waits=w0, inc="pe")
                    wcons[slot] = mmlast[7]
                    if i + 2 < 16:
                        wload(i + 2)
                for c in range(8):
                    bank = p_o[c]
                    lm = em.do("tensor", lambda e, c=c, o=o, bank=bank: e.matmul(
                        bank.ap[:, :], h_bf[:, c * 128:(c + 1) * 128],
                        bcat[:, o * 512:(o + 1) * 512],
                        start=False, stop=False),
                        waits=[bc_t] + hdone, inc="pe")
                    bm = em.do("tensor", lambda e, c=c, o=o, bank=bank: e.matmul(
                        bank.ap[:, :], s_orB[0:1, 0:128],
                        fbrow[0:1, o * 512:(o + 1) * 512],
                        start=False, stop=True),
                        waits=[fb_t, or_t], inc="pe")
                    bank.set_writer(bm)
                    sp = c % 4
                    ev = em.do("scalar", lambda e, c=c, sp=sp, bank=bank: e.copy(
                        stage(sp)[:, :], bank.ap[:, :]),
                        waits=bank.rd_waits() + [stfree[sp]], inc="act")
                    bank.add_reader(ev)
                    qeng, qsem = ("sync", "s") if c % 2 == 0 else ("scalar", "s2")
                    sdma = em.do(qeng, lambda e, c=c, o=o, sp=sp: e.dma_start(
                        out=b.ffn_dram[c * 128:(c + 1) * 128, o * 512:(o + 1) * 512],
                        in_=stage(sp)[:, :]),
                        waits=[ev], inc=qsem, inc_n=16)
                    stfree[sp] = sdma
            b.ffn_done = merge_tok(em.tok("s"), em.tok("s2"))
            if "ffn" in b.taps:
                em.do("sync", lambda e: e.dma_start(out=b.dT["ffn"], in_=b.ffn_dram[:, :]),
                      waits=[b.ffn_done], inc="s", inc_n=16)
            em.do("sync", lambda e: e.nop(),
                  waits=[em.tok("s"), em.tok("s2"), em.tok("w"), em.tok("l")])
        self.run_block(body)
        st.close()

    def phaseG(self):
        b = self
        st = contextlib.ExitStack()
        sb = lambda n, shp, dt: self.sb(n, shp, dt, st)
        arow = sb("arowG", [128, 2, DM], F32)
        xrow = sb("xrowG", [128, 2, DM], F32)
        cols = sb("colsG", [128, 2, 8], F32)

        def body(em):
            prevs = {0: None, 1: None}
            outs = []
            for t in range(NCH):
                prevs[t % 2] = self._emit_ln_rows(em, (arow, xrow, cols, None), t,
                                          b.ffn_dram, b.x1_dram, b.ffn_done,
                                          b.out_t, None, prevs[t % 2])
                outs.append(prevs[t % 2]["out"])
            em.do("sync", lambda e: e.nop(),
                  waits=[em.tok("s"), em.tok("s2"), em.tok("l"), em.tok("w")])
        self.run_block(body)
        st.close()

    def finish_stub(self):
        b = self
        st = contextlib.ExitStack()
        zb = self.sb("zstub", [128, DM], F32, st)

        def body(em):
            z = em.do("gpsimd", lambda e: e.memset(zb[:, :], 0.0), inc="pool")
            for t in range(NCH):
                em.do("sync", lambda e, t=t: e.dma_start(
                    out=b.out_t[t * 128:(t + 1) * 128, :], in_=zb[:, :]),
                    waits=[z], inc="s", inc_n=16)
            em.do("sync", lambda e: e.nop(), waits=[em.tok("s")])
        self.run_block(body)
        st.close()


def build_program(taps=()):
    b = Builder(taps)
    with b.es:
        b.declare()
        b.phase0()
        import os as _os
        lvl = int(_os.environ.get('KLEVEL', '9'))
        b.phaseA(level=min(lvl, 4))
        if lvl >= 5:
            b.yall_stack = contextlib.ExitStack()
            yall = b.sb("yall", [128, H * TPC], BF16, b.yall_stack)
            b.phaseB(yall)
            if lvl >= 6:
                b.phaseC(yall)
            if lvl >= 7:
                b.phaseD(yall)
            b.yall_stack.close()
            if lvl >= 8:
                b.phaseE()
            if lvl >= 9:
                b.phaseF()
                b.phaseG()
        if lvl < 9:
            b.finish_stub()
    return b


def pack_weights(ins):
    Wq = ins["L0_attn_Wq"]; Wk = ins["L0_attn_Wk"]; Wv = ins["L0_attn_Wv"]
    Wo = ins["L0_attn_Wo"]; Wg = ins["L0_attn_Wg"]; Win = ins["L0_attn_Win"]
    Wf = ins["L0_ffn_W"]; lA = ins["L0_ffn_lora_A"]; lB = ins["L0_ffn_lora_B"]
    P = {}
    for nm, W in (("wq", Wq), ("wg", Wg)):
        P[nm] = np.ascontiguousarray(
            W.reshape(H, 128, NDC, 128).transpose(0, 3, 2, 1).reshape(H, 128, DM))
    for nm, W in (("wk", Wk), ("wv", Wv)):
        P[nm] = np.ascontiguousarray(
            W.reshape(2, 512, NDC, 128).transpose(0, 3, 2, 1).reshape(2, 128, NDC * 512))
    P["wo"] = np.ascontiguousarray(
        Wo.reshape(8, 512, H, 128).transpose(0, 3, 2, 1).reshape(8, 128, H * 512))
    P["wf"] = np.ascontiguousarray(
        Wf.reshape(8, 512, NDC, 128).transpose(0, 3, 2, 1).reshape(8, 128, NDC * 512))
    P["win"] = np.ascontiguousarray(
        Win.reshape(32, NDC, 128).transpose(2, 1, 0).reshape(128, NDC * 32))
    A_all = lA.reshape(128, DM)
    P["wa"] = np.ascontiguousarray(
        A_all.reshape(128, NDC, 128).transpose(2, 1, 0).reshape(128, NDC * 128))
    Bcat = lB.transpose(0, 2, 1).reshape(128, DM)
    P["wb"] = np.ascontiguousarray(Bcat.reshape(128, 8, 512).transpose(1, 0, 2))
    P["fbias"] = np.ascontiguousarray(ins["L0_ffn_bias"].reshape(1, DM))
    P["dtb"] = np.ascontiguousarray(ins["L0_attn_dt_bias"].reshape(32, 1))
    P["alog"] = np.ascontiguousarray(ins["L0_attn_A_log_bias"].reshape(32, 1))
    P["gnw"] = np.ascontiguousarray(ins["L0_attn_gnorm_w"].reshape(128, 1))
    P["lwc"] = np.ascontiguousarray(
        np.repeat(ins["L0_ffn_lora_weights"], 16).reshape(128, 1))
    return P


def host_constants():
    C = {}
    C["identc"] = np.eye(128, dtype=BF)
    C["identf"] = np.eye(32, dtype=np.float32)
    idx = np.arange(128)
    mask_ui = (idx[None, :] >= idx[:, None])
    C["negmaskc"] = np.where(mask_ui, np.float32(0), np.float32(-3e38)).astype(np.float32)
    hb = np.zeros((32, H * 128), np.float32)
    for h in range(H):
        hb[h, h * 128:(h + 1) * 128] = 1.0
    C["hbc"] = hb
    C["ones32c"] = np.ones((32, 128), np.float32)
    C["onesrow"] = np.ones((1, 512), np.float32)
    C["onescol_bf"] = np.ones((128, 1), dtype=BF)
    C["epscol"] = np.full((128, 1), 1e-5, dtype=np.float32)
    C["onesrow_bf"] = np.ones((1, 128), dtype=BF)
    return C


def rope_tables(r0):
    pos = np.arange(r0, r0 + TPC, dtype=np.float32)
    inv = 1.0 / 10000.0 ** (np.arange(0, D, 2, dtype=np.float32) / D)
    ang = pos[:, None] * inv[None, :]
    cos = np.concatenate([np.cos(ang), np.cos(ang)], -1).astype(np.float32)
    sin = np.concatenate([np.sin(ang), np.sin(ang)], -1).astype(np.float32)
    return cos, sin


def make_in_maps(inputs):
    P = pack_weights(inputs)
    C = host_constants()
    x = inputs["L0_attn_hidden_states"]
    maps = []
    for core in range(NCORES):
        bb, p = core // 4, core % 4
        r0 = p * TPC
        m = dict(P)
        m.update(C)
        xs = x[bb, r0:r0 + TPC, :]
        m["x_tm"] = np.ascontiguousarray(xs)
        m["x_fm"] = np.ascontiguousarray(
            xs.T.reshape(4, 8, 128, TPC).transpose(0, 2, 1, 3).reshape(4, 128, 8 * TPC))
        cos, sin = rope_tables(r0)
        m["cos_tm"] = cos.astype(BF)
        m["sin_tm"] = sin.astype(BF)
        m["cosT"] = np.ascontiguousarray(cos.T).astype(BF)
        m["sinT"] = np.ascontiguousarray(sin.T).astype(BF)
        mc = np.zeros((128, 4), np.float32)
        mc[:, :p] = 1.0
        m["mcol"] = mc
        m["invmcol"] = np.ascontiguousarray(1.0 - mc)
        maps.append(m)
    return maps


def kernel(**inputs):
    b = build_program()
    in_maps = make_in_maps(inputs)
    res = run_bass_kernel_spmd(b.nc, in_maps, core_ids=list(range(NCORES)))
    out = np.empty((2, 4096, 4096), np.float32)
    for c in range(NCORES):
        out[c // 4, (c % 4) * TPC:(c % 4 + 1) * TPC, :] = res.results[c]["out"]
    return out
